# version 01
# baseline (speedup 1.0000x reference)
import numpy as np
import ml_dtypes
from contextlib import ExitStack

import concourse.bass as bass
import concourse.mybir as mybir
from concourse.bass_utils import run_bass_kernel_spmd

F32 = mybir.dt.float32
BF16 = mybir.dt.bfloat16
I32 = mybir.dt.int32
ALU = mybir.AluOpType
ACT = mybir.ActivationFunctionType
AX = mybir.AxisListType

P = 128
T = 2048
D = 1024
FF = 4096
NT = T // P
NE = D // P
NF = FF // P
NCORE = 8

STAGE = 3
DBG = True
import os as _os
ATTN_PARTS = int(_os.environ.get("ATTN_PARTS", "99"))
ATTN_NG = int(_os.environ.get("ATTN_NG", "4"))


def build():
    nc = bass.Bass("TRN2", target_bir_lowering=False)

    x_in = nc.dram_tensor("x", [T, D], F32, kind="ExternalInput").ap()
    y_out = nc.dram_tensor("y", [T, D], F32, kind="ExternalOutput").ap()
    w1_in = nc.dram_tensor("w1", [FF, D], F32, kind="ExternalInput").ap()
    w2_in = nc.dram_tensor("w2", [D, FF], F32, kind="ExternalInput").ap()
    b1r_in = nc.dram_tensor("b1r", [P, NF], F32, kind="ExternalInput").ap()
    b2r_in = nc.dram_tensor("b2r", [P, NE], F32, kind="ExternalInput").ap()
    wT_in = {w: nc.dram_tensor(w + "T", [D, D], F32, kind="ExternalInput").ap()
             for w in ("wq", "wk", "wv", "wg", "wo")}
    ld0_in = nc.dram_tensor("ld0", [1, 1], F32, kind="ExternalInput").ap()
    diffT_in = nc.dram_tensor("diffT", [P, P], F32, kind="ExternalInput").ap()
    m01_in = nc.dram_tensor("m01", [P, P], F32, kind="ExternalInput").ap()
    id32_in = nc.dram_tensor("id32", [P, P], F32, kind="ExternalInput").ap()

    xint_dram = nc.dram_tensor("xint_dram", [T, D], BF16).ap()
    w1i_dram = nc.dram_tensor("w1i_dram", [FF, D], BF16).ap()
    w2i_dram = nc.dram_tensor("w2i_dram", [D, FF], BF16).ap()
    arelu_dram = nc.dram_tensor("arelu_dram", [FF, T], F32).ap()
    wbf_dram = {w: nc.dram_tensor(w + "_bf", [D, D], BF16).ap() for w in wT_in}
    ccin = [nc.dram_tensor("ccin%d" % i, [1, 1], F32).ap() for i in range(4)]
    ccout = [nc.dram_tensor("ccout%d" % i, [1, 1], F32).ap() for i in range(4)]
    dbg = {}
    if DBG:
        for nm, shp, dt in (("dbg_sw1", [P, NF], F32), ("dbg_bcv", [P, 4], F32),
                            ("dbg_B1", [P, NF], F32), ("dbg_c1s2", [P, NF], F32),
                            ("dbg_c2B", [P, D], F32), ("dbg_d2B", [P, D], F32),
                            ("dbg_xint", [P, D], BF16), ("dbg_h", [P, 512], F32),
                            ("dbg_hint", [P, 2048], BF16), ("dbg_scg", [1, 20], F32),
                            ("dbg_sw2", [P, NE], F32), ("dbg_b2q", [P, NE], F32)):
            dbg[nm] = nc.dram_tensor(nm, shp, dt, kind="ExternalOutput").ap()

    with ExitStack() as stx:
        e = stx.enter_context
        xs = e(nc.sbuf_tensor("xs", [P, NT * D], F32))
        uA = e(nc.sbuf_tensor("uA", [P, 8192], F32))
        xf = e(nc.sbuf_tensor("xf", [P, NE * T], BF16))
        wbuf = e(nc.sbuf_tensor("wbuf", [P, 16384], BF16))
        sp = e(nc.sbuf_tensor("sp", [P, 2048], F32))
        i32b = e(nc.sbuf_tensor("i32b", [P, 2048], I32))
        bfB = e(nc.sbuf_tensor("bfB", [P, 8192], BF16))
        mstat = e(nc.sbuf_tensor("mstat", [P, 16], F32))
        lnst = e(nc.sbuf_tensor("lnst", [P, 256], F32))
        mvec = e(nc.sbuf_tensor("mvec", [P, 3 * NT], F32))
        rvec = e(nc.sbuf_tensor("rvec", [P, 3 * NT], F32))
        sw1 = e(nc.sbuf_tensor("sw1", [P, NF], F32))
        sw2 = e(nc.sbuf_tensor("sw2", [P, NE], F32))
        c1s2 = e(nc.sbuf_tensor("c1s2", [P, NF], F32))
        B1v = e(nc.sbuf_tensor("B1v", [P, NF], F32))
        b1r = e(nc.sbuf_tensor("b1r_s", [P, NF], F32))
        b2r = e(nc.sbuf_tensor("b2r_s", [P, NE], F32))
        b2q = e(nc.sbuf_tensor("b2q", [P, NE], F32))
        c2h = e(nc.sbuf_tensor("c2h", [P, NE], F32))
        d2p = e(nc.sbuf_tensor("d2p", [P, NE], F32))
        row1 = e(nc.sbuf_tensor("row1", [1, 2 * P], F32))
        c2B = e(nc.sbuf_tensor("c2B", [P, D], F32))
        d2B = e(nc.sbuf_tensor("d2B", [P, D], F32))
        umaxv = e(nc.sbuf_tensor("umaxv", [P, NT], F32))
        hmaxc = e(nc.sbuf_tensor("hmaxc", [P, NF * 4], F32))
        hmax2 = e(nc.sbuf_tensor("hmax2", [P, NF], F32))
        red1 = e(nc.sbuf_tensor("red1", [P, 1], F32))
        swi = e(nc.sbuf_tensor("swi", [P, 1], F32))
        scg = e(nc.sbuf_tensor("scg", [1, 20], F32))
        bcv = e(nc.sbuf_tensor("bcv", [P, 4], F32))
        ones_r = e(nc.sbuf_tensor("ones_r", [1, P], F32))
        epsv = e(nc.sbuf_tensor("epsv", [P, 1], F32))
        id32 = e(nc.sbuf_tensor("id32_s", [P, P], F32))
        diffT = e(nc.sbuf_tensor("diffT_s", [P, P], BF16))
        m01 = e(nc.sbuf_tensor("m01_s", [P, P], BF16))
        maskbf = e(nc.sbuf_tensor("maskbf", [P, P], BF16))
        ld0 = e(nc.sbuf_tensor("ld0_s", [1, 1], F32))
        lndb = e(nc.sbuf_tensor("lndb", [P, 1], F32))

        pb = [e(nc.psum_tensor("pb%d" % i, [P, 512], F32)) for i in range(8)]

        SEM = {}
        for s in ("q_x", "q_sm", "q_cl0", "q_cl1", "q_cs0", "q_cs1",
                  "q_w1l0", "q_w1l1", "q_w1s0", "q_w1s1",
                  "q_xi0", "q_xi1", "q_xb", "q_sl1a", "q_sl1b",
                  "q_w2l", "q_w2s0", "q_w2s1",
                  "q_sp0", "q_sp1", "q_sp2", "q_sp3",
                  "q_ar0", "q_ar1", "q_sl2a", "q_sl2b",
                  "q_aw0", "q_aw1", "q_cb", "q_y",
                  "q_cc", "sv", "ss", "stt"):
            SEM[s] = [e(nc.semaphore(s)), 0]

        def inc(inst, name, k=1):
            x = SEM[name]
            inst.then_inc(x[0], k)
            x[1] += k
            return x[1]

        def n(name):
            return SEM[name][1]

        WAITS = []

        def wt(eng, name, val):
            if val and val > 0:
                WAITS.append((name, val))
                eng.wait_ge(SEM[name][0], val)

        block = e(nc.Block())
        sy_, ve_, sc_, te_, gp_ = (block.sync, block.vector, block.scalar,
                                   block.tensor, block.gpsimd)
        M = {}

        def ph0_sync(sync):
            xr = x_in.rearrange("(i p) d -> p i d", p=P)
            for g in range(4):
                inc(sync.dma_start(out=xs[:, g * 4 * D:(g + 1) * 4 * D],
                                   in_=xr[:, g * 4:(g + 1) * 4, :]), "q_x", 16)
            M["x_ld"] = n("q_x")
        sy_(ph0_sync)

        def ph0_gp(gp):
            for dst, src in ((b1r, b1r_in), (b2r, b2r_in), (ld0, ld0_in),
                             (diffT, diffT_in), (m01, m01_in), (id32, id32_in)):
                inc(gp.dma_start(out=dst[:], in_=src), "q_sm", 16)
            M["small_ld"] = n("q_sm")
            if STAGE >= 2:
                k = 0
                for w in ("wq", "wk", "wv", "wg", "wo"):
                    for c in range(NE):
                        par = k % 2
                        stg = bfB[:, par * D:(par + 1) * D]
                        if k >= 2:
                            wt(gp, "q_cs%d" % par, ((k - 2) // 2 + 1) * 16)
                        inc(gp.dma_start(out=stg,
                                         in_=wT_in[w][c * P:(c + 1) * P, :]),
                            "q_cl%d" % par, 16)
                        wt(gp, "q_cl%d" % par, n("q_cl%d" % par))
                        inc(gp.dma_start(out=wbf_dram[w][c * P:(c + 1) * P, :],
                                         in_=stg), "q_cs%d" % par, 16)
                        k += 1
            M["cache_done"] = n("q_cs0") + n("q_cs1")
        gp_(ph0_gp)

        def ph0_v(v):
            v.memset(ones_r[:], 1.0)
            v.memset(bcv[:], 0.0)
            v.memset(scg[:], 0.0)
            inc(v.memset(epsv[:], 1e-5), "sv")
            M["consts"] = n("sv")
        ve_(ph0_v)

        w1_ld, w1_q, w1_cp, w1_st = [], [], [], []
        for i in range(NF):
            slot = i % 2

            def s_load(sync, i=i, slot=slot):
                if i >= 2:
                    wt(sync, "sv", w1_q[i - 2])
                w1_ld.append(inc(sync.dma_start(
                    out=uA[:, 2048 + slot * D:2048 + (slot + 1) * D],
                    in_=w1_in[i * P:(i + 1) * P, :]), "q_w1l%d" % slot, 16))
            sy_(s_load)

            def v_q(v, i=i, slot=slot):
                wt(v, "q_w1l%d" % slot, w1_ld[i])
                stile = uA[:, 2048 + slot * D:2048 + (slot + 1) * D]
                v.tensor_reduce(red1[:], stile, AX.X, ALU.max,
                                apply_absolute_value=True)
                v.drain()
                v.tensor_scalar(sw1[:, i:i + 1], red1[:], 1.0 / 127.0, None,
                                ALU.mult)
                v.drain()
                v.reciprocal(swi[:], sw1[:, i:i + 1])
                v.drain()
                w1_q.append(inc(v.tensor_scalar(
                    i32b[:, slot * D:(slot + 1) * D], stile, swi[:], None,
                    ALU.mult), "sv"))
            ve_(v_q)

            def s_cp(s, i=i, slot=slot):
                wt(s, "sv", w1_q[i])
                if i >= 2:
                    wt(s, "q_w1s%d" % slot, w1_st[i - 2])
                w1_cp.append(inc(s.copy(xf[:, slot * D:(slot + 1) * D],
                                        i32b[:, slot * D:(slot + 1) * D]), "ss"))
            sc_(s_cp)

            def s_st(sync, i=i, slot=slot):
                wt(sync, "ss", w1_cp[i])
                w1_st.append(inc(sync.dma_start(
                    out=w1i_dram[i * P:(i + 1) * P, :],
                    in_=xf[:, slot * D:(slot + 1) * D]), "q_w1s%d" % slot, 16))
            sy_(s_st)

        M["w1s0_end"] = n("q_w1s0")
        M["w1s1_end"] = n("q_w1s1")

        def emit_ln(col0, do_umax, xgate_ms):
            agg3 = lnst[:, 192:256].rearrange("p (t q) -> p t q", q=4)
            mv3 = mvec[:, col0:col0 + 16].rearrange("p (t o) -> p t o", o=1)
            rv3 = rvec[:, col0:col0 + 16].rearrange("p (t o) -> p t o", o=1)
            um3 = umaxv[:, 0:16].rearrange("p (t o) -> p t o", o=1)

            def v_stats(v):
                for i in range(NT):
                    if xgate_ms is not None:
                        wt(v, "sv", xgate_ms[i])
                    if i == 0 and xgate_ms is None:
                        wt(v, "q_x", M["x_ld"])
                    xt = xs[:, i * D:(i + 1) * D]
                    v.bn_stats(lnst[:, 12 * i:12 * i + 6], xt[:, 0:512])
                    v.bn_stats(lnst[:, 12 * i + 6:12 * i + 12], xt[:, 512:1024])
                    if do_umax:
                        v.reduce_max(umaxv[:, i:i + 1], xt, axis=AX.X)
                v.drain()
                for i in range(NT):
                    v.bn_aggr(lnst[:, 192 + 4 * i:192 + 4 * i + 2],
                              lnst[:, 12 * i:12 * i + 6])
                    v.bn_aggr(lnst[:, 192 + 4 * i + 2:192 + 4 * i + 4],
                              lnst[:, 12 * i + 6:12 * i + 12])
                v.drain()
                v.tensor_tensor(mv3, agg3[:, :, 0:1], agg3[:, :, 2:3], ALU.add)
                v.drain()
                v.tensor_scalar(mv3, mv3, 0.5, None, ALU.mult)
                ms3 = mstat[:, 0:16].rearrange("p (t o) -> p t o", o=1)
                v.tensor_tensor(ms3, agg3[:, :, 0:1], agg3[:, :, 2:3],
                                ALU.subtract)
                v.drain()
                v.tensor_scalar(ms3, ms3, 0.5, None, ALU.mult)
                v.drain()
                v.tensor_tensor(ms3, ms3, ms3, ALU.mult)
                v.tensor_tensor(rv3, agg3[:, :, 1:2], agg3[:, :, 3:4], ALU.add)
                v.drain()
                v.tensor_scalar(rv3, rv3, 0.5, None, ALU.mult)
                v.drain()
                inc(v.tensor_tensor(rv3, rv3, ms3, ALU.add), "sv")
            ve_(v_stats)
            m_var = n("sv")

            def s_sq(s):
                wt(s, "sv", m_var)
                inc(s.activation(rvec[:, col0:col0 + 16],
                                 rvec[:, col0:col0 + 16],
                                 ACT.Sqrt, bias=epsv[:]), "ss")
            sc_(s_sq)
            m_sq = n("ss")

            def v_fin(v):
                wt(v, "ss", m_sq)
                v.reciprocal(rvec[:, col0:col0 + 16], rvec[:, col0:col0 + 16])
                if do_umax:
                    v.drain()
                    v.tensor_tensor(umaxv[:, 0:16], umaxv[:, 0:16],
                                    mvec[:, col0:col0 + 16], ALU.subtract)
                    v.drain()
                    v.tensor_tensor(umaxv[:, 0:16], umaxv[:, 0:16],
                                    rvec[:, col0:col0 + 16], ALU.mult)
                    v.drain()
                    v.reduce_max(red1[:], umaxv[:, 0:16], axis=AX.X)
                inc(v.drain(), "sv")
            ve_(v_fin)
            return n("sv")

        def emit_barrier(b, red_ms):
            c0 = 4 * b

            def t_tr(t):
                wt(t, "sv", red_ms)
                wt(t, "q_sm", M["small_ld"])
                inc(t.transpose(pb[7][0:1, 0:P], red1[:, 0:1], id32[:]), "stt")
            te_(t_tr)
            m_tr = n("stt")

            def v_red(v):
                wt(v, "stt", m_tr)
                inc(v.reduce_max(scg[0:1, c0:c0 + 1], pb[7][0:1, 0:P],
                                 axis=AX.X), "sv")
            ve_(v_red)
            m_loc = n("sv")

            def g_cc(gp):
                wt(gp, "sv", m_loc)
                inc(gp.dma_start(out=ccin[b], in_=scg[0:1, c0:c0 + 1]),
                    "q_cb", 16)
                wt(gp, "q_cb", n("q_cb"))
                inc(gp.collective_compute(
                    "AllReduce", ALU.max, replica_groups=[list(range(NCORE))],
                    ins=[ccin[b].opt()], outs=[ccout[b].opt()]), "q_cc", 1)
                wt(gp, "q_cc", n("q_cc"))
                inc(gp.dma_start(out=scg[0:1, c0 + 1:c0 + 2], in_=ccout[b]),
                    "q_cb", 16)
            gp_(g_cc)
            m_cc = n("q_cb")

            def t_bc(t):
                wt(t, "q_cb", m_cc)
                inc(t.matmul(pb[7][:, 0:1], ones_r[0:1, :],
                             scg[0:1, c0 + 1:c0 + 2], start=True, stop=True),
                    "stt")
            te_(t_bc)
            m_bc = n("stt")

            def s_cp(s):
                wt(s, "stt", m_bc)
                inc(s.copy(bcv[:, 2:3], pb[7][:, 0:1]), "ss")
            sc_(s_cp)
            m_gb = n("ss")

            def v_sc(v):
                wt(v, "ss", m_gb)
                v.tensor_scalar(bcv[:, 1:2], bcv[:, 2:3], 1e-8, 1.0 / 255.0,
                                ALU.max, ALU.mult)
                v.drain()
                inc(v.reciprocal(bcv[:, 0:1], bcv[:, 1:2]), "sv")
            ve_(v_sc)
            return n("sv")

        def emit_b1math(mb):
            def v_pb(v):
                wt(v, "sv", mb)
                wt(v, "q_sm", M["small_ld"])
                v.tensor_scalar(c1s2[:], sw1[:], bcv[:, 1:2], None, ALU.mult)
                v.drain()
                v.reciprocal(B1v[:], c1s2[:])
                v.drain()
                v.tensor_tensor(i32b[:, 0:NF], b1r[:], B1v[:], ALU.mult)
                v.drain()
                inc(v.tensor_copy(B1v[:], i32b[:, 0:NF]), "sv")
            ve_(v_pb)
            return n("sv")

        def emit_quant_xbar(col0, quant, gate_sv, xf_gate_stt, xf_gate_dma=()):
            act_ms, cp_ms, st_ms = [], [], []
            for i in range(NT):
                slot = i % 2

                def v_u(v, i=i, slot=slot):
                    wt(v, "sv", gate_sv if isinstance(gate_sv, int)
                       else gate_sv[i])
                    if quant:
                        if i >= 2:
                            wt(v, "ss", act_ms[i - 2])
                        ut = sp[:, slot * D:(slot + 1) * D]
                    else:
                        if i >= 2:
                            wt(v, "q_xi%d" % slot, st_ms[i - 2])
                        if i == 0:
                            wt(v, "stt", xf_gate_stt)
                        ut = bfB[:, 2048 + slot * D:2048 + (slot + 1) * D]
                    inc(v.tensor_scalar(ut, xs[:, i * D:(i + 1) * D],
                                        mvec[:, col0 + i:col0 + i + 1],
                                        rvec[:, col0 + i:col0 + i + 1],
                                        ALU.subtract, ALU.mult), "sv")
                ve_(v_u)
                m_u = n("sv")

                if quant:
                    def s_q(s, i=i, slot=slot, m_u=m_u):
                        wt(s, "sv", m_u)
                        if i >= 2:
                            wt(s, "q_xi%d" % slot, st_ms[i - 2])
                        if i == 0:
                            wt(s, "stt", xf_gate_stt)
                        act_ms.append(inc(s.activation(
                            i32b[:, slot * D:(slot + 1) * D],
                            sp[:, slot * D:(slot + 1) * D],
                            ACT.Relu, scale=bcv[:, 0:1]), "ss"))
                        s.drain()
                        cp_ms.append(inc(s.copy(
                            bfB[:, 2048 + slot * D:2048 + (slot + 1) * D],
                            i32b[:, slot * D:(slot + 1) * D]), "ss"))
                    sc_(s_q)

                def s_st(sync, i=i, slot=slot, m_u=m_u):
                    if quant:
                        wt(sync, "ss", cp_ms[i])
                    else:
                        wt(sync, "sv", m_u)
                    st_ms.append(inc(sync.dma_start(
                        out=xint_dram[i * P:(i + 1) * P, :],
                        in_=bfB[:, 2048 + slot * D:2048 + (slot + 1) * D]),
                        "q_xi%d" % slot, 16))
                sy_(s_st)

            M["qx_st0"] = n("q_xi0")
            M["qx_st1"] = n("q_xi1")
            xb_ms = []

            def s_xb(sync):
                wt(sync, "stt", xf_gate_stt)
                for sem_nm, val in xf_gate_dma:
                    wt(sync, sem_nm, val)
                wt(sync, "q_xi0", M["qx_st0"])
                wt(sync, "q_xi1", M["qx_st1"])
                for c in range(NE):
                    inc(sync.dma_start(out=xf[:, c * T:(c + 1) * T],
                                       in_=xint_dram[:, c * P:(c + 1) * P],
                                       transpose=True), "q_xb", 16)
                    xb_ms.append(n("q_xb"))
            sy_(s_xb)
            return xb_ms

        w2_ld, w2_q, w2_st = [], [], []
        w2_cp = {}

        def w2_load(sync, i):
            if i >= 1:
                wt(sync, "sv", w2_q[2 * i - 1])
            w2_ld.append(inc(sync.dma_start(
                out=uA[:, 4096:8192], in_=w2_in[i * P:(i + 1) * P, :]),
                "q_w2l", 16))

        def w2_quant(v, i, h):
            if h == 0:
                wt(v, "q_w2l", w2_ld[i])
                if i >= 2:
                    wt(v, "ss", w2_cp[(i - 2, 1)])
                else:
                    wt(v, "ss", M.get("qx_ss_end", 0))
                stile = uA[:, 4096:8192]
                v.tensor_reduce(red1[:], stile, AX.X, ALU.max,
                                apply_absolute_value=True)
                v.drain()
                v.tensor_scalar(sw2[:, i:i + 1], red1[:], 1.0 / 127.0, None,
                                ALU.mult)
                v.drain()
                v.reciprocal(swi[:], sw2[:, i:i + 1])
                v.drain()
            else:
                wt(v, "ss", w2_cp[(i, 0)])
            w2_q.append(inc(v.tensor_scalar(
                i32b[:, 0:2048], uA[:, 4096 + h * 2048:4096 + (h + 1) * 2048],
                swi[:], None, ALU.mult), "sv"))

        def w2_copy(s, i, h):
            wt(s, "sv", w2_q[-1])
            slot = i % 2
            if i >= 2 and h == 0:
                wt(s, "q_w2s%d" % slot, w2_st[i - 2])
            if i == 0 and h == 0:
                wt(s, "q_xi0", M.get("qx_st0", 0))
                wt(s, "q_xi1", M.get("qx_st1", 0))
                wt(s, "q_cs0", n("q_cs0"))
                wt(s, "q_cs1", n("q_cs1"))
            w2_cp[(i, h)] = inc(s.copy(
                bfB[:, slot * 4096 + h * 2048:slot * 4096 + (h + 1) * 2048],
                i32b[:, 0:2048]), "ss")

        def w2_store(sync, i):
            wt(sync, "ss", w2_cp[(i, 1)])
            slot = i % 2
            w2_st.append(inc(sync.dma_start(
                out=w2i_dram[i * P:(i + 1) * P, :],
                in_=bfB[:, slot * 4096:(slot + 1) * 4096]),
                "q_w2s%d" % slot, 16))

        def emit_mm1(call, xb_ms, b1_ms, mask_gate_ss, wbuf_gate_stt, w2_ss_gate=0):
            slab_ms = [0] * 8
            slab_end = [0] * 8
            ev_ms = {}
            spw_ms = {}
            vx_ms = {}

            def slab_load(sync, sl):
                slot = sl % 2
                if sl >= 2:
                    wt(sync, "stt", slab_end[sl - 2])
                if sl < 2:
                    wt(sync, "q_w1s0", M.get("w1s0_end", 0))
                    wt(sync, "q_w1s1", M.get("w1s1_end", 0))
                sem = "q_sl1a" if slot == 0 else "q_sl1b"
                for c in range(NE):
                    inc(sync.dma_start(
                        out=wbuf[:, slot * 4096 + c * 512:
                                 slot * 4096 + (c + 1) * 512],
                        in_=w1i_dram[sl * 512:(sl + 1) * 512,
                                     c * P:(c + 1) * P],
                        transpose=True), sem, 16)
                slab_ms[sl] = (sem, n(sem))

            def s_pre(sync):
                wt(sync, "stt", wbuf_gate_stt)
                slab_load(sync, 0)
                slab_load(sync, 1)
            sy_(s_pre)

            mm_ms = {}
            for f in range(NF):
                sl = f // 4
                slot = sl % 2
                grp = (f % 2) * 4

                def t_mm(t, f=f, sl=sl, slot=slot, grp=grp):
                    wt(t, slab_ms[sl][0], slab_ms[sl][1])
                    if f == 0:
                        wt(t, "q_xb", xb_ms[-1])
                        wt(t, "sv", b1_ms)
                    if f == 1 and mask_gate_ss:
                        wt(t, "ss", mask_gate_ss)
                    if f >= 2:
                        wt(t, "ss", ev_ms[(f - 2, 3)])
                    for ei in range(NE):
                        for tt in range(4):
                            mm = t.matmul(
                                pb[grp + tt][:, 0:512],
                                wbuf[:, slot * 4096 + ei * 512 + (f % 4) * 128:
                                     slot * 4096 + ei * 512 + (f % 4) * 128 + 128],
                                xf[:, ei * T + tt * 512:ei * T + (tt + 1) * 512],
                                start=(ei == 0), stop=(ei == NE - 1))
                            if ei == NE - 1:
                                mm_ms[(f, tt)] = inc(mm, "stt")
                    slab_end[sl] = n("stt")
                te_(t_mm)

                def s_ev(s, f=f, grp=grp):
                    if f == 0:
                        wt(s, "sv", b1_ms)
                    for tt in range(4):
                        j = f * 4 + tt
                        wt(s, "stt", mm_ms[(f, tt)])
                        if j >= 4:
                            pf, ptt = (j - 4) // 4, (j - 4) % 4
                            wt(s, "sv", vx_ms[(pf, ptt)])
                            wt(s, "q_sp%d" % (j % 4), spw_ms[(pf, ptt)])
                        ev_ms[(f, tt)] = inc(s.activation(
                            sp[:, (j % 4) * 512:(j % 4 + 1) * 512],
                            pb[grp + tt][:, 0:512],
                            ACT.Relu, bias=B1v[:, f:f + 1]), "ss")
                sc_(s_ev)

                def v_mx(v, f=f):
                    for tt in range(4):
                        j = f * 4 + tt
                        wt(v, "ss", ev_ms[(f, tt)])
                        vx_ms[(f, tt)] = inc(v.reduce_max(
                            hmaxc[:, j:j + 1],
                            sp[:, (j % 4) * 512:(j % 4 + 1) * 512],
                            axis=AX.X), "sv")
                ve_(v_mx)

                def g_sp(gp, f=f):
                    for tt in range(4):
                        j = f * 4 + tt
                        wt(gp, "ss", ev_ms[(f, tt)])
                        spw_ms[(f, tt)] = inc(gp.dma_start(
                            out=arelu_dram[f * P:(f + 1) * P,
                                           tt * 512:(tt + 1) * 512],
                            in_=sp[:, (j % 4) * 512:(j % 4 + 1) * 512]),
                            "q_sp%d" % (j % 4), 16)
                gp_(g_sp)

                if f % 4 == 3 and f // 4 + 2 < 8:
                    def s_sl(sync, sl2=f // 4 + 2):
                        slab_load(sync, sl2)
                    sy_(s_sl)

                if call == 0:
                    lds = (0, 3, 6, 9, 12, 15, 18, 21)
                    qts = (2, 5, 8, 11, 14, 17, 20, 23)
                    if f in lds:
                        def s_w2l(sync, i=lds.index(f)):
                            w2_load(sync, i)
                        sy_(s_w2l)
                    if f in qts:
                        iq = qts.index(f)
                        for h in range(2):
                            def v_w2(v, i=iq, h=h):
                                w2_quant(v, i, h)
                            ve_(v_w2)

                            def s_w2c(s, i=iq, h=h):
                                w2_copy(s, i, h)
                            sc_(s_w2c)

                        def s_w2s(sync, i=iq):
                            w2_store(sync, i)
                        sy_(s_w2s)

            def v_hm(v):
                v.drain()
                for fc in range(NF):
                    v.reduce_max(hmax2[:, fc:fc + 1],
                                 hmaxc[:, fc * 4:(fc + 1) * 4], axis=AX.X)
                v.drain()
                v.tensor_tensor(hmax2[:], hmax2[:], c1s2[:], ALU.mult)
                v.drain()
                inc(v.reduce_max(red1[:], hmax2[:], axis=AX.X), "sv")
            ve_(v_hm)
            spl = tuple(n("q_sp%d" % k) for k in range(4))
            return n("sv"), n("stt"), spl

        def emit_pb2(m_b2):
            def v_m(v):
                wt(v, "sv", m_b2)
                v.tensor_scalar(c1s2[:], c1s2[:], bcv[:, 0:1], None, ALU.mult)
                v.tensor_scalar(c2h[:], sw2[:], bcv[:, 1:2], 0.5,
                                ALU.mult, ALU.mult)
                v.tensor_scalar(d2p[:], sw2[:], bcv[:, 1:2], None, ALU.mult)
                v.drain()
                v.reciprocal(b2q[:], d2p[:])
                v.drain()
                v.tensor_tensor(i32b[:, 0:NE], b2r[:], b2q[:], ALU.mult)
                v.drain()
                v.tensor_copy(b2q[:], i32b[:, 0:NE])
                v.drain()
                inc(v.tensor_tensor(d2p[:], b2q[:], c2h[:], ALU.mult), "sv")
            ve_(v_m)
            m_vm = n("sv")

            bc_cp = {}
            for dc in range(NE):
                def t_tr2(t, dc=dc):
                    if dc == 0:
                        wt(t, "sv", m_vm)
                    if dc >= 1:
                        wt(t, "ss", bc_cp[("r", dc - 1)])
                    t.transpose(pb[4][0:1, 0:P], c2h[:, dc:dc + 1], id32[:])
                    bc_cp[("t", dc)] = inc(t.transpose(
                        pb[5][0:1, 0:P], d2p[:, dc:dc + 1], id32[:]), "stt")
                te_(t_tr2)

                def s_r(s, dc=dc):
                    wt(s, "stt", bc_cp[("t", dc)])
                    s.copy(row1[0:1, 0:P], pb[4][0:1, 0:P])
                    bc_cp[("r", dc)] = inc(s.copy(row1[0:1, P:2 * P],
                                                  pb[5][0:1, 0:P]), "ss")
                sc_(s_r)

                def t_bc2(t, dc=dc):
                    wt(t, "ss", bc_cp[("r", dc)])
                    if dc >= 1:
                        wt(t, "ss", bc_cp[dc - 1])
                    t.matmul(pb[6][:, 0:P], ones_r[0:1, :], row1[0:1, 0:P],
                             start=True, stop=True)
                    bc_cp[("mm", dc)] = inc(t.matmul(
                        pb[7][:, 0:P], ones_r[0:1, :], row1[0:1, P:2 * P],
                        start=True, stop=True), "stt")
                te_(t_bc2)

                def s_bc2(s, dc=dc):
                    wt(s, "stt", bc_cp[("mm", dc)])
                    s.copy(c2B[:, dc * P:(dc + 1) * P], pb[6][:, 0:P])
                    bc_cp[dc] = inc(s.copy(d2B[:, dc * P:(dc + 1) * P],
                                           pb[7][:, 0:P]), "ss")
                sc_(s_bc2)
            return n("ss")

        def emit_mm2(call, m_pb2, store_y, spill_total, mm1_end):
            ar_ms = {}
            rq_ms = {}
            hi_ms = {}
            grp_end = {}
            ev1max = {}
            add_ms = {}
            sl_ms = {}

            for g in range(4):
                for ti in range(4):
                    t_idx = g * 4 + ti
                    slot = t_idx % 2

                    def g_ld(gp, t_idx=t_idx, slot=slot):
                        if t_idx == 0:
                            for k in range(4):
                                wt(gp, "q_sp%d" % k, spill_total[k])
                            wt(gp, "ss", m_pb2)
                        if t_idx >= 2:
                            wt(gp, "sv", rq_ms[(t_idx - 2, 1)])
                        ar = arelu_dram.rearrange("(fc p) t -> p fc t", p=P)
                        ar_ms[t_idx] = inc(gp.dma_start(
                            out=uA[:, slot * 4096:(slot + 1) * 4096],
                            in_=ar[:, :, t_idx * P:(t_idx + 1) * P]),
                            "q_ar%d" % slot, 16)
                    gp_(g_ld)

                    for h in range(2):
                        def v_rq(v, t_idx=t_idx, slot=slot, h=h):
                            if h == 0:
                                wt(v, "q_ar%d" % slot, ar_ms[t_idx])
                                if t_idx >= 1:
                                    wt(v, "ss", hi_ms[(t_idx - 1, 1)])
                            else:
                                wt(v, "ss", hi_ms[(t_idx, 0)])
                            mm = None
                            for fc in range(16):
                                fca = h * 16 + fc
                                mm = v.tensor_scalar(
                                    i32b[:, fc * P:(fc + 1) * P],
                                    uA[:, slot * 4096 + fca * P:
                                       slot * 4096 + (fca + 1) * P],
                                    c1s2[:, fca:fca + 1], None, ALU.mult)
                            rq_ms[(t_idx, h)] = inc(mm, "sv")
                        ve_(v_rq)

                        def s_hi(s, t_idx=t_idx, ti=ti, g=g, h=h):
                            wt(s, "sv", rq_ms[(t_idx, h)])
                            if g >= 1 and h == 0:
                                wt(s, "stt", grp_end[g - 1])
                            hi_ms[(t_idx, h)] = inc(s.copy(
                                xf[:, ti * 4096 + h * 2048:
                                   ti * 4096 + (h + 1) * 2048],
                                i32b[:, 0:2048]), "ss")
                        sc_(s_hi)

                for fc in range(NF):
                    slot = fc % 2

                    def s_sl(sync, g=g, fc=fc, slot=slot):
                        if fc >= 2:
                            wt(sync, "stt", sl_ms.get(("end", g, fc - 2), 0))
                        elif g >= 1:
                            wt(sync, "stt", grp_end[g - 1])
                        else:
                            wt(sync, "stt", mm1_end)
                            if g == 0:
                                wt(sync, "q_w2s0", n("q_w2s0"))
                                wt(sync, "q_w2s1", n("q_w2s1"))
                        sem = "q_sl2a" if slot == 0 else "q_sl2b"
                        sl_ms[(g, fc)] = inc(sync.dma_start(
                            out=wbuf[:, slot * 1024:(slot + 1) * 1024],
                            in_=w2i_dram[:, fc * P:(fc + 1) * P],
                            transpose=True), sem, 16)
                    sy_(s_sl)

                    def t_fc(t, g=g, fc=fc, slot=slot):
                        if fc == 0:
                            wt(t, "ss", hi_ms[(g * 4 + 3, 1)])
                            if g >= 1:
                                wt(t, "sv", ev1max[g - 1])
                        wt(t, "q_sl2a" if slot == 0 else "q_sl2b",
                           sl_ms[(g, fc)])
                        for ti in range(4):
                            for dh in range(2):
                                mm = t.matmul(
                                    pb[ti * 2 + dh][:, 0:512],
                                    xf[:, ti * 4096 + fc * P:
                                       ti * 4096 + (fc + 1) * P],
                                    wbuf[:, slot * 1024 + dh * 512:
                                         slot * 1024 + (dh + 1) * 512],
                                    start=(fc == 0), stop=(fc == NF - 1))
                        sl_ms[("end", g, fc)] = inc(mm, "stt")
                    te_(t_fc)
                grp_end[g] = n("stt")

                def v_ev(v, g=g):
                    wt(v, "stt", grp_end[g])
                    for half in range(2):
                        tis = (0, 1) if half == 0 else (2, 3)
                        for ti in tis:
                            for dh in range(2):
                                ssl = (ti % 2) * 2 + dh
                                v.tensor_tensor(
                                    sp[:, ssl * 512:(ssl + 1) * 512],
                                    pb[ti * 2 + dh][:, 0:512],
                                    c2B[:, dh * 512:(dh + 1) * 512], ALU.mult)
                        mev = inc(v.drain(), "sv")
                        ev1max[g] = mev
                        for ti in tis:
                            for dh in range(2):
                                ssl = (ti % 2) * 2 + dh
                                v.tensor_tensor(
                                    sp[:, ssl * 512:(ssl + 1) * 512],
                                    sp[:, ssl * 512:(ssl + 1) * 512],
                                    d2B[:, dh * 512:(dh + 1) * 512], ALU.add)
                        v.drain()
                        for ti in tis:
                            t_idx = g * 4 + ti
                            for dh in range(2):
                                ssl = (ti % 2) * 2 + dh
                                add_ms[t_idx] = inc(v.tensor_tensor(
                                    xs[:, t_idx * D + dh * 512:
                                       t_idx * D + (dh + 1) * 512],
                                    xs[:, t_idx * D + dh * 512:
                                       t_idx * D + (dh + 1) * 512],
                                    sp[:, ssl * 512:(ssl + 1) * 512],
                                    ALU.add), "sv")
                        if half == 0:
                            v.drain()
                ve_(v_ev)

                if store_y:
                    def s_y(sync, g=g):
                        for ti in range(4):
                            t_idx = g * 4 + ti
                            wt(sync, "sv", add_ms[t_idx])
                            inc(sync.dma_start(
                                out=y_out[t_idx * P:(t_idx + 1) * P, :],
                                in_=xs[:, t_idx * D:(t_idx + 1) * D]),
                                "q_y", 16)
                    sy_(s_y)

            return [add_ms[i] for i in range(NT)]

        emit_ln(0, True, None)
        mb1 = emit_barrier(0, n("sv"))
        b1_ms = emit_b1math(mb1)

        mask_gate = 0
        if STAGE >= 2:
            def s_mask1(s):
                wt(s, "q_sm", M["small_ld"])
                s.activation(scg[0:1, 16:17], ld0[0:1, 0:1], ACT.Sigmoid)
                s.drain()
                inc(s.activation(scg[0:1, 17:18], scg[0:1, 16:17], ACT.Ln),
                    "ss")
            sc_(s_mask1)
            m_mk1 = n("ss")

            def t_mask(t):
                wt(t, "ss", m_mk1)
                inc(t.matmul(pb[6][:, 0:1], ones_r[0:1, :], scg[0:1, 17:18],
                             start=True, stop=True), "stt")
            te_(t_mask)
            m_mk2 = n("stt")

            def s_mask2(s):
                wt(s, "stt", m_mk2)
                m_lnd = inc(s.copy(lndb[:], pb[6][:, 0:1]), "ss")
                s.drain()
                inc(s.activation(uA[:, 0:P], diffT[:], ACT.Exp, scale=lndb[:]),
                    "ss")
                M["m_lnd"] = m_lnd
            sc_(s_mask2)
            m_mk3 = n("ss")
            mask_gate = M["m_lnd"]

            def v_mask(v):
                wt(v, "ss", m_mk3)
                wt(v, "q_sm", M["small_ld"])
                inc(v.tensor_tensor(maskbf[:], uA[:, 0:P], m01[:], ALU.mult),
                    "sv")
            ve_(v_mask)
            M["mask_rdy"] = n("sv")

        xb0 = emit_quant_xbar(0, True, b1_ms, 0,
                              xf_gate_dma=(("q_w1s0", M["w1s0_end"]),
                                           ("q_w1s1", M["w1s1_end"])))
        M["qx_ss_end"] = n("ss")
        hm0, mm1_end0, spl0 = emit_mm1(0, xb0, b1_ms, mask_gate, 0)
        mb2 = emit_barrier(1, hm0)
        mpb2 = emit_pb2(mb2)
        add_ffn1 = emit_mm2(0, mpb2, STAGE == 1, spl0, mm1_end0)
        mm2_end0 = n("stt")
        M["mm2end0"] = mm2_end0
        m_mm2ev0 = n("sv")

        if STAGE == 1:
            def s_fin(sync):
                if DBG:
                    for src_ap, nm in ((sw1[:], "dbg_sw1"), (bcv[:], "dbg_bcv"),
                                       (B1v[:], "dbg_B1"), (c1s2[:], "dbg_c1s2"),
                                       (c2B[:], "dbg_c2B"), (d2B[:], "dbg_d2B"),
                                       (xf[:, 0:2048], "dbg_hint"),
                                       (scg[:], "dbg_scg"), (sw2[:], "dbg_sw2"),
                                       (b2q[:], "dbg_b2q")):
                        inc(sync.dma_start(out=dbg[nm], in_=src_ap), "q_y", 16)
                    inc(sync.dma_start(out=dbg["dbg_xint"],
                                       in_=xint_dram[0:P, :]), "q_y", 16)
                    inc(sync.dma_start(out=dbg["dbg_h"],
                                       in_=arelu_dram[0:P, 0:512]), "q_y", 16)
                wt(sync, "q_y", n("q_y"))
            sy_(s_fin)
            _check(SEM, WAITS)
            return nc

        emit_ln(NT, False, add_ffn1)
        xb_u2 = emit_quant_xbar(NT, False, add_ffn1, mm2_end0)

        AG, AQ, AK, AV, ASM = 0, 0, 4096, 8192, 12288
        uB = uA[:, :].bitcast(BF16)
        WORDER = ("wq", "wk", "wv", "wg", "wo")
        wv_ms = {}
        wlast = {}
        res_ms = {}

        def aw_load(sync, g, widx):
            k = g * 5 + widx
            slot = k % 2
            if k >= 2:
                wt(sync, "stt", wlast.get(k - 2, 0))
            if k == 0:
                wt(sync, "q_cs0", n("q_cs0"))
                wt(sync, "q_cs1", n("q_cs1"))
            if k < 2:
                wt(sync, "stt", M["mm2end0"])
            wr = wbf_dram[WORDER[widx]].rearrange("(c p) d -> p c d", p=P)
            inc(sync.dma_start(
                out=wbuf[:, slot * 8192:slot * 8192 + 4096],
                in_=wr[:, 0:4, :]), "q_aw%d" % slot, 16)
            wv_ms[k] = inc(sync.dma_start(
                out=wbuf[:, slot * 8192 + 4096:(slot + 1) * 8192],
                in_=wr[:, 4:8, :]), "q_aw%d" % slot, 16)

        def a_ld(sync):
            aw_load(sync, 0, 0)
        sy_(a_ld)

        for g in range(ATTN_NG):
            q_ev, k_ev, g_ev, v_ev2 = {}, {}, {}, {}
            stm_ms, go_ms, o_ms, st_att = {}, {}, {}, {}

            def a_ld2(sync, g=g):
                aw_load(sync, g, 1)
            sy_(a_ld2)

            def proj_feat(g, widx, dstbase, bank0, sig, ev, first_gate_ss,
                          first_gate_sv, in_uB=False):
                k = g * 5 + widx
                slot = k % 2
                for dc in range(NE):
                    def t_p(t, dc=dc, k=k, slot=slot, g=g, widx=widx,
                            bank0=bank0):
                        bk = bank0 + (dc % 2)
                        if dc == 0:
                            wt(t, "q_aw%d" % (k % 2), wv_ms[k])
                            wt(t, "q_xb", xb_u2[-1])
                            if first_gate_ss:
                                wt(t, "ss", first_gate_ss)
                            if first_gate_sv:
                                wt(t, "sv", first_gate_sv)
                        if dc >= 2:
                            wt(t, "ss", ev[dc - 2])
                        for ei in range(NE):
                            mm = t.matmul(
                                pb[bk][:, 0:512],
                                wbuf[:, slot * 8192 + ei * D + dc * P:
                                     slot * 8192 + ei * D + (dc + 1) * P],
                                xf[:, ei * T + g * 512:ei * T + (g + 1) * 512],
                                start=(ei == 0), stop=(ei == NE - 1))
                        ev[("mm", dc)] = inc(mm, "stt")
                        wlast[k] = n("stt")
                    te_(t_p)

                    def s_p(s, dc=dc, dstbase=dstbase, bank0=bank0, sig=sig,
                            ev=ev, in_uB=in_uB):
                        bk = bank0 + (dc % 2)
                        wt(s, "stt", ev[("mm", dc)])
                        buf = uB if in_uB else bfB
                        dst = buf[:, dstbase + dc * 512:dstbase + (dc + 1) * 512]
                        if sig:
                            op = s.activation(dst, pb[bk][:, 0:512], ACT.Sigmoid)
                        else:
                            op = s.copy(dst, pb[bk][:, 0:512])
                        ev[dc] = inc(op, "ss")
                    sc_(s_p)

            gate0_sv = m_mm2ev0 if g == 0 else res_ms.get((g - 1) * 4 + 3, m_mm2ev0)
            if ATTN_PARTS < 1:
                continue
            proj_feat(g, 0, AQ, 0, False, q_ev, 0, gate0_sv, in_uB=True)

            def a_ld3(sync, g=g):
                aw_load(sync, g, 2)
            sy_(a_ld3)

            proj_feat(g, 1, AK, 2, False, k_ev, 0, gate0_sv, in_uB=True)

            def a_ld4(sync, g=g):
                aw_load(sync, g, 3)
            sy_(a_ld4)

            k = g * 5 + 2
            slotv = k % 2
            for tsub in range(4):
                for dh in range(2):
                    def t_v(t, g=g, tsub=tsub, dh=dh, k=k, slotv=slotv,
                            gate0_sv=gate0_sv):
                        bk = 4 + (tsub % 2) * 2 + dh
                        if tsub == 0 and dh == 0:
                            wt(t, "q_aw%d" % (k % 2), wv_ms[k])
                            wt(t, "sv", gate0_sv)
                        if tsub >= 2:
                            wt(t, "ss", v_ev2[(tsub - 2, dh)])
                        for ei in range(NE):
                            mm = t.matmul(
                                pb[bk][:, 0:512],
                                xf[:, ei * T + g * 512 + tsub * P:
                                   ei * T + g * 512 + (tsub + 1) * P],
                                wbuf[:, slotv * 8192 + ei * D + dh * 512:
                                     slotv * 8192 + ei * D + (dh + 1) * 512],
                                start=(ei == 0), stop=(ei == NE - 1))
                        v_ev2[("mm", tsub, dh)] = inc(mm, "stt")
                        wlast[k] = n("stt")
                    te_(t_v)

                    def s_v(s, g=g, tsub=tsub, dh=dh):
                        bk = 4 + (tsub % 2) * 2 + dh
                        wt(s, "stt", v_ev2[("mm", tsub, dh)])
                        v_ev2[(tsub, dh)] = inc(s.copy(
                            uB[:, AV + tsub * D + dh * 512:
                               AV + tsub * D + (dh + 1) * 512],
                            pb[bk][:, 0:512]), "ss")
                    sc_(s_v)

            def a_ld5(sync, g=g):
                aw_load(sync, g, 4)
            sy_(a_ld5)

            proj_feat(g, 3, AG, 0, True, g_ev, q_ev[NE - 1], 0)

            if ATTN_PARTS < 2:
                continue
            for w in range(4):
                def t_st(t, g=g, w=w):
                    wt(t, "ss", max(q_ev[NE - 1], k_ev[NE - 1]))
                    wt(t, "ss", v_ev2[(3, 0)])
                    if w >= 1:
                        wt(t, "sv", max(go_ms.get((w - 1, 0), stm_ms[w - 1]),
                                        go_ms.get((w - 1, 1), stm_ms[w - 1])))
                    mm = None
                    for dc in range(NE):
                        mm = t.matmul(
                            pb[6][:, 0:P],
                            uB[:, AK + dc * 512 + w * P:AK + dc * 512 + (w + 1) * P],
                            uB[:, AQ + dc * 512 + w * P:AQ + dc * 512 + (w + 1) * P],
                            start=(dc == 0), stop=(dc == NE - 1))
                    st_att[w] = inc(mm, "stt")
                te_(t_st)

                def v_stm(v, g=g, w=w):
                    wt(v, "stt", st_att[w])
                    if w == 0 and g == 0:
                        wt(v, "sv", M["mask_rdy"])
                    stm_ms[w] = inc(v.tensor_tensor(
                        uB[:, ASM + w * P:ASM + (w + 1) * P],
                        pb[6][:, 0:P], maskbf[:], ALU.mult), "sv")
                ve_(v_stm)

                if ATTN_PARTS < 3:
                    continue
                for half in range(2):
                    def t_o(t, g=g, w=w, half=half):
                        bnk = 7 if half == 0 else 6
                        if half == 0:
                            wt(t, "sv", stm_ms[w])
                            wt(t, "ss", v_ev2[(3, 1)])
                            if w >= 1:
                                wt(t, "sv", go_ms[(w - 1, 1)])
                        else:
                            wt(t, "sv", stm_ms[w])
                            if w >= 1:
                                wt(t, "sv", go_ms[(w - 1, 0)])
                        mm = None
                        for k in range(4):
                            dc = half * 4 + k
                            mm = t.matmul(
                                pb[bnk][:, k * P:(k + 1) * P],
                                uB[:, AV + w * D + dc * P:AV + w * D + (dc + 1) * P],
                                uB[:, ASM + w * P:ASM + (w + 1) * P],
                                start=True, stop=True)
                        o_ms[(w, half)] = inc(mm, "stt")
                    te_(t_o)

                    def v_go(v, g=g, w=w, half=half):
                        bnk = 7 if half == 0 else 6
                        wt(v, "stt", o_ms[(w, half)])
                        if w == 0 and half == 0:
                            wt(v, "ss", g_ev[NE - 1])
                        for k in range(4):
                            dc = half * 4 + k
                            gsl = bfB[:, AG + dc * 512 + w * P:
                                      AG + dc * 512 + (w + 1) * P]
                            mm = v.tensor_tensor(gsl, pb[bnk][:, k * P:(k + 1) * P],
                                                 gsl, ALU.mult)
                        go_ms[(w, half)] = inc(mm, "sv")
                    ve_(v_go)

            if ATTN_PARTS < 4:
                continue
            k = g * 5 + 4
            slotw = k % 2
            womm = {}
            for tsub in range(4):
                def t_wo(t, g=g, tsub=tsub, k=k, slotw=slotw):
                    if tsub == 0:
                        wt(t, "q_aw%d" % (k % 2), wv_ms[k])
                        wt(t, "sv", max(go_ms[(w, h)] for w in range(4)
                                        for h in range(2)))
                        wt(t, "ss", g_ev[NE - 1])
                    if tsub >= 2:
                        wt(t, "sv", res_ms[("h", g * 4 + tsub - 2, 0)])
                        wt(t, "sv", res_ms[("h", g * 4 + tsub - 2, 1)])
                    for dh in range(2):
                        bk = (tsub % 2) * 2 + dh
                        for dc in range(NE):
                            mm = t.matmul(
                                pb[bk][:, 0:512],
                                bfB[:, AG + dc * 512 + tsub * P:
                                     AG + dc * 512 + (tsub + 1) * P],
                                wbuf[:, slotw * 8192 + dc * D + dh * 512:
                                     slotw * 8192 + dc * D + (dh + 1) * 512],
                                start=(dc == 0), stop=(dc == NE - 1))
                        womm[(tsub, dh)] = inc(mm, "stt")
                    wlast[k] = n("stt")
                te_(t_wo)

                def v_res(v, g=g, tsub=tsub):
                    t_idx = g * 4 + tsub
                    ut = sp[:, (tsub % 2) * D:(tsub % 2 + 1) * D]
                    wt(v, "stt", womm[(tsub, 1)])
                    v.tensor_scalar(ut, xs[:, t_idx * D:(t_idx + 1) * D],
                                    mvec[:, NT + t_idx:NT + t_idx + 1],
                                    rvec[:, NT + t_idx:NT + t_idx + 1],
                                    ALU.subtract, ALU.mult)
                    v.drain()
                    for dh in range(2):
                        bk = (tsub % 2) * 2 + dh
                        res_ms[("h", t_idx, dh)] = inc(v.tensor_tensor(
                            ut[:, dh * 512:(dh + 1) * 512],
                            pb[bk][:, 0:512],
                            ut[:, dh * 512:(dh + 1) * 512], ALU.add), "sv")
                    v.drain()
                    res_ms[t_idx] = inc(v.tensor_tensor(
                        xs[:, t_idx * D:(t_idx + 1) * D],
                        xs[:, t_idx * D:(t_idx + 1) * D], ut, ALU.add), "sv")
                ve_(v_res)

            if g < 3:
                def a_ld6(sync, g=g):
                    aw_load(sync, g + 1, 0)
                sy_(a_ld6)

        attn_end = n("stt")
        res_list = [res_ms.get(i, n("sv")) for i in range(NT)]

        if STAGE == 2:
            def s_fin2(sync):
                for i in range(NT):
                    wt(sync, "sv", res_list[i])
                    inc(sync.dma_start(out=y_out[i * P:(i + 1) * P, :],
                                       in_=xs[:, i * D:(i + 1) * D]),
                        "q_y", 16)
                wt(sync, "q_y", n("q_y"))
            sy_(s_fin2)
            _check(SEM, WAITS)
            return nc

        emit_ln(2 * NT, True, res_list)
        mb3 = emit_barrier(2, n("sv"))
        b1_ms2 = emit_b1math(mb3)
        xb1 = emit_quant_xbar(2 * NT, True, b1_ms2, attn_end)
        hm1, mm1_end1, spl1 = emit_mm1(1, xb1, b1_ms2, 0, attn_end)
        mb4 = emit_barrier(3, hm1)
        mpb4 = emit_pb2(mb4)
        emit_mm2(1, mpb4, True, spl1, mm1_end1)

        def s_fin3(sync):
            wt(sync, "q_y", n("q_y"))
        sy_(s_fin3)
        _check(SEM, WAITS)

    return nc


_CACHE = {}


def _prep(inputs):
    xfull = np.ascontiguousarray(np.asarray(inputs["L0_attn_x"], dtype=np.float32))
    w1 = np.ascontiguousarray(np.asarray(inputs["L0_ffn_w1"], dtype=np.float32))
    w2 = np.ascontiguousarray(np.asarray(inputs["L0_ffn_w2"], dtype=np.float32))
    b1 = np.asarray(inputs["L0_ffn_b1"], dtype=np.float32)
    b2 = np.asarray(inputs["L0_ffn_b2"], dtype=np.float32)
    b1r = np.ascontiguousarray(b1.reshape(NF, P).T)
    b2r = np.ascontiguousarray(b2.reshape(NE, P).T)
    wT = {w: np.ascontiguousarray(np.asarray(inputs[k], dtype=np.float32).T)
          for w, k in (("wq", "L0_attn_W_q"), ("wk", "L0_attn_W_k"),
                       ("wv", "L0_attn_W_v"), ("wg", "L0_attn_W_g"),
                       ("wo", "L0_attn_W_o"))}
    ld0 = np.ascontiguousarray(
        np.asarray(inputs["L0_attn_log_decay"], dtype=np.float32)[0:1]
        .reshape(1, 1))
    kk = np.arange(P)
    cc = np.arange(P)
    diff = cc[None, :] - kk[:, None]
    valid = (cc[None, :] // 32 == kk[:, None] // 32) & (diff >= 0)
    diffT = np.where(valid, diff, 0).astype(np.float32)
    m01v = valid.astype(np.float32)
    id32v = np.eye(P, dtype=np.float32)
    per_core = []
    for c in range(NCORE):
        m = {"x": np.ascontiguousarray(xfull[c]), "w1": w1, "w2": w2,
             "b1r": b1r, "b2r": b2r, "ld0": ld0, "diffT": diffT,
             "m01": m01v, "id32": id32v}
        for w in wT:
            m[w + "T"] = wT[w]
        per_core.append(m)
    return per_core


def _check(SEM, WAITS):
    bad = [(nm, v, SEM[nm][1]) for nm, v in WAITS if v > SEM[nm][1]]
    if bad:
        raise RuntimeError("unsatisfiable waits: %s" % bad[:10])


def kernel(**inputs):
    if "nc" not in _CACHE:
        _CACHE["nc"] = build()
    nc = _CACHE["nc"]
    in_maps = _prep(inputs)
    res = run_bass_kernel_spmd(nc, in_maps, core_ids=list(range(NCORE)))
    return np.stack([res.results[i]["y"] for i in range(NCORE)], axis=0)


# version 02 (final)
# speedup vs baseline: 1.0131x; 1.0131x over previous
import numpy as np
import ml_dtypes
from contextlib import ExitStack

import concourse.bass as bass
import concourse.mybir as mybir
from concourse.bass_utils import run_bass_kernel_spmd

F32 = mybir.dt.float32
BF16 = mybir.dt.bfloat16
I32 = mybir.dt.int32
ALU = mybir.AluOpType
ACT = mybir.ActivationFunctionType
AX = mybir.AxisListType

P = 128
T = 2048
D = 1024
FF = 4096
NT = T // P
NE = D // P
NF = FF // P
NCORE = 8

STAGE = 3
DBG = True
import os as _os
ATTN_PARTS = int(_os.environ.get("ATTN_PARTS", "99"))
ATTN_NG = int(_os.environ.get("ATTN_NG", "4"))


def build():
    nc = bass.Bass("TRN2", target_bir_lowering=False)

    x_in = nc.dram_tensor("x", [T, D], F32, kind="ExternalInput").ap()
    y_out = nc.dram_tensor("y", [T, D], F32, kind="ExternalOutput").ap()
    w1_in = nc.dram_tensor("w1", [FF, D], F32, kind="ExternalInput").ap()
    w2_in = nc.dram_tensor("w2", [D, FF], F32, kind="ExternalInput").ap()
    b1r_in = nc.dram_tensor("b1r", [P, NF], F32, kind="ExternalInput").ap()
    b2r_in = nc.dram_tensor("b2r", [P, NE], F32, kind="ExternalInput").ap()
    wT_in = {w: nc.dram_tensor(w + "T", [D, D], F32, kind="ExternalInput").ap()
             for w in ("wq", "wk", "wv", "wg", "wo")}
    ld0_in = nc.dram_tensor("ld0", [1, 1], F32, kind="ExternalInput").ap()
    diffT_in = nc.dram_tensor("diffT", [P, P], F32, kind="ExternalInput").ap()
    m01_in = nc.dram_tensor("m01", [P, P], F32, kind="ExternalInput").ap()
    id32_in = nc.dram_tensor("id32", [P, P], F32, kind="ExternalInput").ap()

    xint_dram = nc.dram_tensor("xint_dram", [T, D], BF16).ap()
    w1i_dram = nc.dram_tensor("w1i_dram", [FF, D], BF16).ap()
    w2i_dram = nc.dram_tensor("w2i_dram", [D, FF], BF16).ap()
    arelu_dram = nc.dram_tensor("arelu_dram", [FF, T], F32).ap()
    w1iT_dram = nc.dram_tensor("w1iT_dram", [D, FF], BF16).ap()
    w2iT_dram = nc.dram_tensor("w2iT_dram", [FF, D], BF16).ap()
    wbf_dram = {w: nc.dram_tensor(w + "_bf", [D, D], BF16).ap() for w in wT_in}
    ccin = [nc.dram_tensor("ccin%d" % i, [1, 1], F32).ap() for i in range(4)]
    ccout = [nc.dram_tensor("ccout%d" % i, [1, 1], F32).ap() for i in range(4)]
    dbg = {}
    if DBG:
        for nm, shp, dt in (("dbg_sw1", [P, NF], F32), ("dbg_bcv", [P, 4], F32),
                            ("dbg_B1", [P, NF], F32), ("dbg_c1s2", [P, NF], F32),
                            ("dbg_c2B", [P, D], F32), ("dbg_d2B", [P, D], F32),
                            ("dbg_xint", [P, D], BF16), ("dbg_h", [P, 512], F32),
                            ("dbg_hint", [P, 2048], BF16), ("dbg_scg", [1, 20], F32),
                            ("dbg_sw2", [P, NE], F32), ("dbg_b2q", [P, NE], F32)):
            dbg[nm] = nc.dram_tensor(nm, shp, dt, kind="ExternalOutput").ap()

    with ExitStack() as stx:
        e = stx.enter_context
        xs = e(nc.sbuf_tensor("xs", [P, NT * D], F32))
        uA = e(nc.sbuf_tensor("uA", [P, 8192], F32))
        xf = e(nc.sbuf_tensor("xf", [P, NE * T], BF16))
        wbuf = e(nc.sbuf_tensor("wbuf", [P, 16384], BF16))
        sp = e(nc.sbuf_tensor("sp", [P, 2048], F32))
        i32b = e(nc.sbuf_tensor("i32b", [P, 2048], I32))
        bfB = e(nc.sbuf_tensor("bfB", [P, 8192], BF16))
        mstat = e(nc.sbuf_tensor("mstat", [P, 16], F32))
        lnst = e(nc.sbuf_tensor("lnst", [P, 256], F32))
        mvec = e(nc.sbuf_tensor("mvec", [P, 3 * NT], F32))
        rvec = e(nc.sbuf_tensor("rvec", [P, 3 * NT], F32))
        sw1 = e(nc.sbuf_tensor("sw1", [P, NF], F32))
        sw2 = e(nc.sbuf_tensor("sw2", [P, NE], F32))
        c1s2 = e(nc.sbuf_tensor("c1s2", [P, NF], F32))
        B1v = e(nc.sbuf_tensor("B1v", [P, NF], F32))
        b1r = e(nc.sbuf_tensor("b1r_s", [P, NF], F32))
        b2r = e(nc.sbuf_tensor("b2r_s", [P, NE], F32))
        b2q = e(nc.sbuf_tensor("b2q", [P, NE], F32))
        c2h = e(nc.sbuf_tensor("c2h", [P, NE], F32))
        d2p = e(nc.sbuf_tensor("d2p", [P, NE], F32))
        row1 = e(nc.sbuf_tensor("row1", [1, 2 * P], F32))
        c2B = e(nc.sbuf_tensor("c2B", [P, D], F32))
        d2B = e(nc.sbuf_tensor("d2B", [P, D], F32))
        umaxv = e(nc.sbuf_tensor("umaxv", [P, NT], F32))
        hmaxc = e(nc.sbuf_tensor("hmaxc", [P, NF * 4], F32))
        hmax2 = e(nc.sbuf_tensor("hmax2", [P, NF], F32))
        red1 = e(nc.sbuf_tensor("red1", [P, 1], F32))
        swi = e(nc.sbuf_tensor("swi", [P, 1], F32))
        scg = e(nc.sbuf_tensor("scg", [1, 20], F32))
        bcv = e(nc.sbuf_tensor("bcv", [P, 4], F32))
        ones_r = e(nc.sbuf_tensor("ones_r", [1, P], F32))
        epsv = e(nc.sbuf_tensor("epsv", [P, 1], F32))
        id32 = e(nc.sbuf_tensor("id32_s", [P, P], F32))
        diffT = e(nc.sbuf_tensor("diffT_s", [P, P], BF16))
        m01 = e(nc.sbuf_tensor("m01_s", [P, P], BF16))
        maskbf = e(nc.sbuf_tensor("maskbf", [P, P], BF16))
        ld0 = e(nc.sbuf_tensor("ld0_s", [1, 1], F32))
        lndb = e(nc.sbuf_tensor("lndb", [P, 1], F32))

        pb = [e(nc.psum_tensor("pb%d" % i, [P, 512], F32)) for i in range(8)]

        SEM = {}
        for s in ("q_x", "q_sm", "q_cl0", "q_cl1", "q_cs0", "q_cs1",
                  "q_w1l0", "q_w1l1", "q_w1s0", "q_w1s1",
                  "q_xi0", "q_xi1", "q_xb", "q_sl1a", "q_sl1b",
                  "q_w2l", "q_w2s0", "q_w2s1",
                  "q_sp0", "q_sp1", "q_sp2", "q_sp3", "q_w1t0", "q_w1t1",
                  "q_w2t0", "q_w2t1", "q_w2t2", "q_w2t3",
                  "q_ar0", "q_ar1", "q_sl2a", "q_sl2b", "q_sl2c", "q_sl2d",
                  "q_aw0", "q_aw1", "q_cb", "q_y",
                  "q_cc", "sv", "ss", "stt"):
            SEM[s] = [e(nc.semaphore(s)), 0]

        def inc(inst, name, k=1):
            x = SEM[name]
            inst.then_inc(x[0], k)
            x[1] += k
            return x[1]

        def n(name):
            return SEM[name][1]

        WAITS = []

        def wt(eng, name, val):
            if val and val > 0:
                WAITS.append((name, val))
                eng.wait_ge(SEM[name][0], val)

        block = e(nc.Block())
        sy_, ve_, sc_, te_, gp_ = (block.sync, block.vector, block.scalar,
                                   block.tensor, block.gpsimd)
        M = {}

        def ph0_sync(sync):
            xr = x_in.rearrange("(i p) d -> p i d", p=P)
            for g in range(4):
                inc(sync.dma_start(out=xs[:, g * 4 * D:(g + 1) * 4 * D],
                                   in_=xr[:, g * 4:(g + 1) * 4, :]), "q_x", 16)
            M["x_ld"] = n("q_x")
        sy_(ph0_sync)

        def ph0_gp(gp):
            for dst, src in ((b1r, b1r_in), (b2r, b2r_in), (ld0, ld0_in),
                             (diffT, diffT_in), (m01, m01_in), (id32, id32_in)):
                inc(gp.dma_start(out=dst[:], in_=src), "q_sm", 16)
            M["small_ld"] = n("q_sm")
            if STAGE >= 2:
                k = 0
                for w in ("wq", "wk", "wv", "wg", "wo"):
                    for c in range(NE):
                        par = k % 2
                        stg = bfB[:, par * D:(par + 1) * D]
                        if k >= 2:
                            wt(gp, "q_cs%d" % par, ((k - 2) // 2 + 1) * 16)
                        inc(gp.dma_start(out=stg,
                                         in_=wT_in[w][c * P:(c + 1) * P, :]),
                            "q_cl%d" % par, 16)
                        wt(gp, "q_cl%d" % par, n("q_cl%d" % par))
                        inc(gp.dma_start(out=wbf_dram[w][c * P:(c + 1) * P, :],
                                         in_=stg), "q_cs%d" % par, 16)
                        k += 1
            M["cache_done"] = n("q_cs0") + n("q_cs1")
        gp_(ph0_gp)

        def ph0_v(v):
            v.memset(ones_r[:], 1.0)
            v.memset(bcv[:], 0.0)
            v.memset(scg[:], 0.0)
            inc(v.memset(epsv[:], 1e-5), "sv")
            M["consts"] = n("sv")
        ve_(ph0_v)

        def emit_ln(col0, do_umax, xgate_ms):
            agg3 = lnst[:, 192:256].rearrange("p (t q) -> p t q", q=4)
            mv3 = mvec[:, col0:col0 + 16].rearrange("p (t o) -> p t o", o=1)
            rv3 = rvec[:, col0:col0 + 16].rearrange("p (t o) -> p t o", o=1)
            um3 = umaxv[:, 0:16].rearrange("p (t o) -> p t o", o=1)

            def v_stats(v):
                for i in range(NT):
                    if xgate_ms is not None:
                        wt(v, "sv", xgate_ms[i])
                    if i == 0 and xgate_ms is None:
                        wt(v, "q_x", M["x_ld"])
                    xt = xs[:, i * D:(i + 1) * D]
                    v.bn_stats(lnst[:, 12 * i:12 * i + 6], xt[:, 0:512])
                    v.bn_stats(lnst[:, 12 * i + 6:12 * i + 12], xt[:, 512:1024])
                    if do_umax:
                        v.reduce_max(umaxv[:, i:i + 1], xt, axis=AX.X)
                v.drain()
                for i in range(NT):
                    v.bn_aggr(lnst[:, 192 + 4 * i:192 + 4 * i + 2],
                              lnst[:, 12 * i:12 * i + 6])
                    v.bn_aggr(lnst[:, 192 + 4 * i + 2:192 + 4 * i + 4],
                              lnst[:, 12 * i + 6:12 * i + 12])
                v.drain()
                v.tensor_tensor(mv3, agg3[:, :, 0:1], agg3[:, :, 2:3], ALU.add)
                v.drain()
                v.tensor_scalar(mv3, mv3, 0.5, None, ALU.mult)
                ms3 = mstat[:, 0:16].rearrange("p (t o) -> p t o", o=1)
                v.tensor_tensor(ms3, agg3[:, :, 0:1], agg3[:, :, 2:3],
                                ALU.subtract)
                v.drain()
                v.tensor_scalar(ms3, ms3, 0.5, None, ALU.mult)
                v.drain()
                v.tensor_tensor(ms3, ms3, ms3, ALU.mult)
                v.tensor_tensor(rv3, agg3[:, :, 1:2], agg3[:, :, 3:4], ALU.add)
                v.drain()
                v.tensor_scalar(rv3, rv3, 0.5, None, ALU.mult)
                v.drain()
                inc(v.tensor_tensor(rv3, rv3, ms3, ALU.add), "sv")
            ve_(v_stats)
            m_var = n("sv")

            def s_sq(s):
                wt(s, "sv", m_var)
                inc(s.activation(rvec[:, col0:col0 + 16],
                                 rvec[:, col0:col0 + 16],
                                 ACT.Sqrt, bias=epsv[:]), "ss")
            sc_(s_sq)
            m_sq = n("ss")

            def v_fin(v):
                wt(v, "ss", m_sq)
                v.reciprocal(rvec[:, col0:col0 + 16], rvec[:, col0:col0 + 16])
                if do_umax:
                    v.drain()
                    v.tensor_tensor(umaxv[:, 0:16], umaxv[:, 0:16],
                                    mvec[:, col0:col0 + 16], ALU.subtract)
                    v.drain()
                    v.tensor_tensor(umaxv[:, 0:16], umaxv[:, 0:16],
                                    rvec[:, col0:col0 + 16], ALU.mult)
                    v.drain()
                    v.reduce_max(red1[:], umaxv[:, 0:16], axis=AX.X)
                inc(v.drain(), "sv")
            ve_(v_fin)
            return n("sv")

        def emit_barrier_pre(b, red_ms):
            c0 = 4 * b

            def t_tr(t):
                wt(t, "sv", red_ms)
                wt(t, "q_sm", M["small_ld"])
                inc(t.transpose(pb[7][0:1, 0:P], red1[:, 0:1], id32[:]), "stt")
            te_(t_tr)
            m_tr = n("stt")

            def v_red(v):
                wt(v, "stt", m_tr)
                inc(v.reduce_max(scg[0:1, c0:c0 + 1], pb[7][0:1, 0:P],
                                 axis=AX.X), "sv")
            ve_(v_red)
            return n("sv")

        def emit_barrier_post(b, m_loc, warm=150):
            c0 = 4 * b

            def g_cc(gp):
                wt(gp, "sv", m_loc)
                inc(gp.dma_start(out=ccin[b], in_=scg[0:1, c0:c0 + 1]),
                    "q_cb", 16)
                wt(gp, "q_cb", n("q_cb"))
                inc(gp.collective_compute(
                    "AllReduce", ALU.max, replica_groups=[list(range(NCORE))],
                    ins=[ccin[b].opt()], outs=[ccout[b].opt()]), "q_cc", 1)
                wt(gp, "q_cc", n("q_cc"))
                inc(gp.dma_start(out=scg[0:1, c0 + 1:c0 + 2], in_=ccout[b]),
                    "q_cb", 16)
            gp_(g_cc)
            m_cc = n("q_cb")

            def t_bc(t):
                for _ in range(warm):
                    t.matmul(pb[5][0:1, 0:P], ones_r[0:1, 0:1],
                             ones_r[0:1, 0:P], start=True, stop=True)
                wt(t, "q_cb", m_cc)
                inc(t.matmul(pb[7][:, 0:1], ones_r[0:1, :],
                             scg[0:1, c0 + 1:c0 + 2], start=True, stop=True),
                    "stt")
            te_(t_bc)
            m_bc = n("stt")

            def s_cp(s):
                wt(s, "stt", m_bc)
                inc(s.copy(bcv[:, 2:3], pb[7][:, 0:1]), "ss")
            sc_(s_cp)
            m_gb = n("ss")

            def v_sc(v):
                wt(v, "ss", m_gb)
                v.tensor_scalar(bcv[:, 1:2], bcv[:, 2:3], 1e-8, 1.0 / 255.0,
                                ALU.max, ALU.mult)
                v.drain()
                inc(v.reciprocal(bcv[:, 0:1], bcv[:, 1:2]), "sv")
            ve_(v_sc)
            return n("sv")

        def emit_barrier(b, red_ms):
            return emit_barrier_post(b, emit_barrier_pre(b, red_ms))

        def emit_b1math(mb):
            def v_pb(v):
                wt(v, "sv", mb)
                wt(v, "q_sm", M["small_ld"])
                v.tensor_scalar(c1s2[:], sw1[:], bcv[:, 1:2], 1.0 / 127.0,
                                ALU.mult, ALU.mult)
                v.drain()
                v.reciprocal(B1v[:], c1s2[:])
                v.drain()
                v.tensor_tensor(i32b[:, 0:NF], b1r[:], B1v[:], ALU.mult)
                v.drain()
                inc(v.tensor_copy(B1v[:], i32b[:, 0:NF]), "sv")
            ve_(v_pb)
            return n("sv")

        def emit_quant_xbar(col0, quant, gate_sv, xf_gate_stt, xf_gate_dma=()):
            act_ms, cp_ms, st_ms = [], [], []
            for i in range(NT):
                slot = i % 2

                def v_u(v, i=i, slot=slot):
                    wt(v, "sv", gate_sv if isinstance(gate_sv, int)
                       else gate_sv[i])
                    if quant:
                        if i >= 2:
                            wt(v, "ss", act_ms[i - 2])
                        ut = sp[:, slot * D:(slot + 1) * D]
                    else:
                        if i >= 2:
                            wt(v, "q_xi%d" % slot, st_ms[i - 2])
                        if i == 0:
                            wt(v, "stt", xf_gate_stt)
                        ut = bfB[:, 2048 + slot * D:2048 + (slot + 1) * D]
                    inc(v.tensor_scalar(ut, xs[:, i * D:(i + 1) * D],
                                        mvec[:, col0 + i:col0 + i + 1],
                                        rvec[:, col0 + i:col0 + i + 1],
                                        ALU.subtract, ALU.mult), "sv")
                ve_(v_u)
                m_u = n("sv")

                if quant:
                    def s_q(s, i=i, slot=slot, m_u=m_u):
                        wt(s, "sv", m_u)
                        if i >= 2:
                            wt(s, "q_xi%d" % slot, st_ms[i - 2])
                        if i == 0:
                            wt(s, "stt", xf_gate_stt)
                        act_ms.append(inc(s.activation(
                            i32b[:, slot * D:(slot + 1) * D],
                            sp[:, slot * D:(slot + 1) * D],
                            ACT.Relu, scale=bcv[:, 0:1]), "ss"))
                        s.drain()
                        cp_ms.append(inc(s.copy(
                            bfB[:, 2048 + slot * D:2048 + (slot + 1) * D],
                            i32b[:, slot * D:(slot + 1) * D]), "ss"))
                    sc_(s_q)

                def s_st(sync, i=i, slot=slot, m_u=m_u):
                    if quant:
                        wt(sync, "ss", cp_ms[i])
                    else:
                        wt(sync, "sv", m_u)
                    st_ms.append(inc(sync.dma_start(
                        out=xint_dram[i * P:(i + 1) * P, :],
                        in_=bfB[:, 2048 + slot * D:2048 + (slot + 1) * D]),
                        "q_xi%d" % slot, 16))
                sy_(s_st)

            M["qx_st0"] = n("q_xi0")
            M["qx_st1"] = n("q_xi1")
            xb_ms = []

            def s_xb(sync):
                wt(sync, "stt", xf_gate_stt)
                for sem_nm, val in xf_gate_dma:
                    wt(sync, sem_nm, val)
                wt(sync, "q_xi0", M["qx_st0"])
                wt(sync, "q_xi1", M["qx_st1"])
                for c in range(NE):
                    inc(sync.dma_start(out=xf[:, c * T:(c + 1) * T],
                                       in_=xint_dram[:, c * P:(c + 1) * P],
                                       transpose=True), "q_xb", 16)
                    xb_ms.append(n("q_xb"))
            sy_(s_xb)
            return xb_ms

        w2_ld, w2_q, w2_st = [], [], []
        w2_cp = {}

        def w2_load(sync, i):
            if i >= 1:
                wt(sync, "sv", w2_q[2 * i - 1])
            w2_ld.append(inc(sync.dma_start(
                out=uA[:, 4096:8192], in_=w2_in[i * P:(i + 1) * P, :]),
                "q_w2l", 16))

        def w2_quant(v, i, h):
            if h == 0:
                wt(v, "q_w2l", w2_ld[i])
                if i >= 2:
                    wt(v, "ss", w2_cp[(i - 2, 1)])
                else:
                    wt(v, "ss", M.get("qx_ss_end", 0))
                stile = uA[:, 4096:8192]
                v.tensor_reduce(sw2[:, i:i + 1], stile, AX.X, ALU.max,
                                apply_absolute_value=True)
                v.drain()
                v.reciprocal(swi[:], sw2[:, i:i + 1])
                v.drain()
            else:
                wt(v, "ss", w2_cp[(i, 0)])
            w2_q.append(inc(v.tensor_scalar(
                i32b[:, 0:2048], uA[:, 4096 + h * 2048:4096 + (h + 1) * 2048],
                swi[:], 127.0, ALU.mult, ALU.mult), "sv"))

        def w2_copy(s, i, h):
            wt(s, "sv", w2_q[-1])
            slot = i % 2
            if i >= 2 and h == 0:
                wt(s, "q_w2s%d" % slot, w2_st[i - 2])
            if i == 0 and h == 0:
                wt(s, "q_xi0", M.get("qx_st0", 0))
                wt(s, "q_xi1", M.get("qx_st1", 0))
                wt(s, "q_cs0", n("q_cs0"))
                wt(s, "q_cs1", n("q_cs1"))
            w2_cp[(i, h)] = inc(s.copy(
                bfB[:, slot * 4096 + h * 2048:slot * 4096 + (h + 1) * 2048],
                i32b[:, 0:2048]), "ss")

        def w2_store(sync, i):
            wt(sync, "ss", w2_cp[(i, 1)])
            slot = i % 2
            w2_st.append(inc(sync.dma_start(
                out=w2i_dram[i * P:(i + 1) * P, :],
                in_=bfB[:, slot * 4096:(slot + 1) * 4096]),
                "q_w2s%d" % slot, 16))

        def emit_mm1(call, xb_ms, b1_ms, mask_gate_ss, wbuf_gate_stt, w2_ss_gate=0):
            slab_ms = [0] * 8
            slab_end = [0] * 8
            slab_st = [0] * 8
            ev_ms = {}
            spw_ms = {}
            vx_ms = {}

            def slab_load(sync, sl):
                slot = sl % 2
                if sl >= 2:
                    wt(sync, "stt", slab_end[sl - 2])
                    if call == 0:
                        wt(sync, "q_w1t%d" % (sl % 2), slab_st[sl - 2])
                if sl < 2:
                    wt(sync, "stt", wbuf_gate_stt)
                    if call == 0:
                        wt(sync, "q_w1s0", M.get("w1s0_end", 0))
                        wt(sync, "q_w1s1", M.get("w1s1_end", 0))
                    else:
                        wt(sync, "q_w1t0", M.get("w1t_end0", 0))
                        wt(sync, "q_w1t1", M.get("w1t_end1", 0))
                sem = "q_sl1a" if slot == 0 else "q_sl1b"
                if call == 0:
                    for c in range(NE):
                        inc(sync.dma_start(
                            out=wbuf[:, slot * 4096 + c * 512:
                                     slot * 4096 + (c + 1) * 512],
                            in_=w1i_dram[sl * 512:(sl + 1) * 512,
                                         c * P:(c + 1) * P],
                            transpose=True), sem, 16)
                    slab_ms[sl] = (sem, n(sem))
                    wt(sync, sem, n(sem))
                    wr = w1iT_dram.rearrange("(c p) f -> p c f", p=P)
                    for h in range(2):
                        inc(sync.dma_start(
                            out=wr[:, h * 4:(h + 1) * 4,
                                   sl * 512:(sl + 1) * 512],
                            in_=wbuf[:, slot * 4096 + h * 2048:
                                     slot * 4096 + (h + 1) * 2048]
                            .rearrange("p (c f) -> p c f", c=4)),
                            "q_w1t%d" % (sl % 2), 16)
                    slab_st[sl] = n("q_w1t%d" % (sl % 2))
                else:
                    wr = w1iT_dram.rearrange("(c p) f -> p c f", p=P)
                    for h in range(2):
                        inc(sync.dma_start(
                            out=wbuf[:, slot * 4096 + h * 2048:
                                     slot * 4096 + (h + 1) * 2048]
                            .rearrange("p (c f) -> p c f", c=4),
                            in_=wr[:, h * 4:(h + 1) * 4,
                                   sl * 512:(sl + 1) * 512]), sem, 16)
                    slab_ms[sl] = (sem, n(sem))

            def s_pre(sync):
                slab_load(sync, 0)
                slab_load(sync, 1)
            sy_(s_pre)

            def t_warm(t):
                for _ in range(300):
                    t.matmul(pb[5][0:1, 0:P], ones_r[0:1, 0:1],
                             ones_r[0:1, 0:P], start=True, stop=True)
            te_(t_warm)

            mm_ms = {}
            for f in range(NF):
                sl = f // 4
                slot = sl % 2
                grp = (f % 2) * 4

                def t_mm(t, f=f, sl=sl, slot=slot, grp=grp):
                    wt(t, slab_ms[sl][0], slab_ms[sl][1])
                    if f == 0:
                        wt(t, "q_xb", xb_ms[-1])
                        wt(t, "sv", b1_ms)
                    if f == 1 and mask_gate_ss:
                        wt(t, "ss", mask_gate_ss)
                    if f >= 2:
                        wt(t, "ss", ev_ms[(f - 2, 3)])
                    for ei in range(NE):
                        for tt in range(4):
                            mm = t.matmul(
                                pb[grp + tt][:, 0:512],
                                wbuf[:, slot * 4096 + ei * 512 + (f % 4) * 128:
                                     slot * 4096 + ei * 512 + (f % 4) * 128 + 128],
                                xf[:, ei * T + tt * 512:ei * T + (tt + 1) * 512],
                                start=(ei == 0), stop=(ei == NE - 1))
                            if ei == NE - 1:
                                mm_ms[(f, tt)] = inc(mm, "stt")
                    slab_end[sl] = n("stt")
                te_(t_mm)

                def s_ev(s, f=f, grp=grp):
                    if f == 0:
                        wt(s, "sv", b1_ms)
                    for tt in range(4):
                        j = f * 4 + tt
                        wt(s, "stt", mm_ms[(f, tt)])
                        if j >= 4:
                            pf, ptt = (j - 4) // 4, (j - 4) % 4
                            wt(s, "sv", vx_ms[(pf, ptt)])
                            wt(s, "q_sp%d" % (j % 4), spw_ms[(pf, ptt)])
                        ev_ms[(f, tt)] = inc(s.activation(
                            sp[:, (j % 4) * 512:(j % 4 + 1) * 512],
                            pb[grp + tt][:, 0:512],
                            ACT.Relu, bias=B1v[:, f:f + 1]), "ss")
                sc_(s_ev)

                def v_mx(v, f=f):
                    for tt in range(4):
                        j = f * 4 + tt
                        wt(v, "ss", ev_ms[(f, tt)])
                        vx_ms[(f, tt)] = inc(v.reduce_max(
                            hmaxc[:, j:j + 1],
                            sp[:, (j % 4) * 512:(j % 4 + 1) * 512],
                            axis=AX.X), "sv")
                ve_(v_mx)

                if f % 4 == 3 and f // 4 + 2 < 8:
                    def s_sl(sync, sl2=f // 4 + 2):
                        slab_load(sync, sl2)
                    sy_(s_sl)

                def s_sp2(sync, f=f):
                    for tt in range(4):
                        j = f * 4 + tt
                        wt(sync, "ss", ev_ms[(f, tt)])
                        spw_ms[(f, tt)] = inc(sync.dma_start(
                            out=arelu_dram[f * P:(f + 1) * P,
                                           tt * 512:(tt + 1) * 512],
                            in_=sp[:, (j % 4) * 512:(j % 4 + 1) * 512]),
                            "q_sp%d" % (j % 4), 16)
                sy_(s_sp2)

                if call == 0:
                    lds = (0, 3, 6, 9, 12, 15, 18, 21)
                    qts = (2, 5, 8, 11, 14, 17, 20, 23)
                    if f in lds:
                        def s_w2l(sync, i=lds.index(f)):
                            w2_load(sync, i)
                        sy_(s_w2l)
                    if f in qts:
                        iq = qts.index(f)
                        for h in range(2):
                            def v_w2(v, i=iq, h=h):
                                w2_quant(v, i, h)
                            ve_(v_w2)

                            def s_w2c(s, i=iq, h=h):
                                w2_copy(s, i, h)
                            sc_(s_w2c)

                        def s_w2s(sync, i=iq):
                            w2_store(sync, i)
                        sy_(s_w2s)

            def v_hm(v):
                v.drain()
                for fc in range(NF):
                    v.reduce_max(hmax2[:, fc:fc + 1],
                                 hmaxc[:, fc * 4:(fc + 1) * 4], axis=AX.X)
                v.drain()
                v.tensor_tensor(hmax2[:], hmax2[:], c1s2[:], ALU.mult)
                v.drain()
                inc(v.reduce_max(red1[:], hmax2[:], axis=AX.X), "sv")
            ve_(v_hm)
            if call == 0:
                M["w1t_end0"] = n("q_w1t0")
                M["w1t_end1"] = n("q_w1t1")
            spl = tuple(n("q_sp%d" % k) for k in range(4))
            return n("sv"), n("stt"), spl

        def emit_pb2(m_b2):
            def v_m(v):
                wt(v, "sv", m_b2)
                v.tensor_scalar(c1s2[:], c1s2[:], bcv[:, 0:1], None, ALU.mult)
                v.tensor_scalar(c2h[:], sw2[:], bcv[:, 1:2], 0.5 / 127.0,
                                ALU.mult, ALU.mult)
                v.tensor_scalar(d2p[:], sw2[:], bcv[:, 1:2], 1.0 / 127.0,
                                ALU.mult, ALU.mult)
                v.drain()
                v.reciprocal(b2q[:], d2p[:])
                v.drain()
                v.tensor_tensor(i32b[:, 0:NE], b2r[:], b2q[:], ALU.mult)
                v.drain()
                v.tensor_copy(b2q[:], i32b[:, 0:NE])
                v.drain()
                inc(v.tensor_tensor(d2p[:], b2q[:], c2h[:], ALU.mult), "sv")
            ve_(v_m)
            m_vm = n("sv")

            bc_cp = {}
            for dc in range(NE):
                def t_tr2(t, dc=dc):
                    if dc == 0:
                        wt(t, "sv", m_vm)
                    if dc >= 1:
                        wt(t, "ss", bc_cp[("r", dc - 1)])
                    t.transpose(pb[4][0:1, 0:P], c2h[:, dc:dc + 1], id32[:])
                    bc_cp[("t", dc)] = inc(t.transpose(
                        pb[5][0:1, 0:P], d2p[:, dc:dc + 1], id32[:]), "stt")
                te_(t_tr2)

                def s_r(s, dc=dc):
                    wt(s, "stt", bc_cp[("t", dc)])
                    s.copy(row1[0:1, 0:P], pb[4][0:1, 0:P])
                    bc_cp[("r", dc)] = inc(s.copy(row1[0:1, P:2 * P],
                                                  pb[5][0:1, 0:P]), "ss")
                sc_(s_r)

                def t_bc2(t, dc=dc):
                    wt(t, "ss", bc_cp[("r", dc)])
                    if dc >= 1:
                        wt(t, "ss", bc_cp[dc - 1])
                    t.matmul(pb[6][:, 0:P], ones_r[0:1, :], row1[0:1, 0:P],
                             start=True, stop=True)
                    bc_cp[("mm", dc)] = inc(t.matmul(
                        pb[7][:, 0:P], ones_r[0:1, :], row1[0:1, P:2 * P],
                        start=True, stop=True), "stt")
                te_(t_bc2)

                def s_bc2(s, dc=dc):
                    wt(s, "stt", bc_cp[("mm", dc)])
                    s.copy(c2B[:, dc * P:(dc + 1) * P], pb[6][:, 0:P])
                    bc_cp[dc] = inc(s.copy(d2B[:, dc * P:(dc + 1) * P],
                                           pb[7][:, 0:P]), "ss")
                sc_(s_bc2)
            return n("ss")

        def emit_mm2(call, m_pb2, store_y, spill_total, mm1_end):
            ar_ms = {}
            rq_ms = {}
            hi_ms = {}
            grp_end = {}
            ev1max = {}
            add_ms = {}
            sl_ms = {}

            for g in range(4):
                for ti in range(4):
                    t_idx = g * 4 + ti
                    slot = t_idx % 2

                    def g_ld(gp, t_idx=t_idx, slot=slot):
                        if t_idx == 0:
                            for k in range(4):
                                wt(gp, "q_sp%d" % k, spill_total[k])
                            wt(gp, "ss", m_pb2)
                        if t_idx >= 2:
                            wt(gp, "sv", rq_ms[(t_idx - 2, 1)])
                        ar = arelu_dram.rearrange("(fc p) t -> p fc t", p=P)
                        ar_ms[t_idx] = inc(gp.dma_start(
                            out=uA[:, slot * 4096:(slot + 1) * 4096],
                            in_=ar[:, :, t_idx * P:(t_idx + 1) * P]),
                            "q_ar%d" % slot, 16)
                    gp_(g_ld)

                    for h in range(2):
                        def v_rq(v, t_idx=t_idx, slot=slot, h=h):
                            if h == 0:
                                wt(v, "q_ar%d" % slot, ar_ms[t_idx])
                                if t_idx >= 1:
                                    wt(v, "ss", hi_ms[(t_idx - 1, 1)])
                            else:
                                wt(v, "ss", hi_ms[(t_idx, 0)])
                            mm = None
                            for fc in range(16):
                                fca = h * 16 + fc
                                mm = v.tensor_scalar(
                                    i32b[:, fc * P:(fc + 1) * P],
                                    uA[:, slot * 4096 + fca * P:
                                       slot * 4096 + (fca + 1) * P],
                                    c1s2[:, fca:fca + 1], None, ALU.mult)
                            rq_ms[(t_idx, h)] = inc(mm, "sv")
                        ve_(v_rq)

                        def s_hi(s, t_idx=t_idx, ti=ti, g=g, h=h):
                            wt(s, "sv", rq_ms[(t_idx, h)])
                            if g >= 1 and h == 0:
                                wt(s, "stt", grp_end[g - 1])
                            hi_ms[(t_idx, h)] = inc(s.copy(
                                xf[:, ti * 4096 + h * 2048:
                                   ti * 4096 + (h + 1) * 2048],
                                i32b[:, 0:2048]), "ss")
                        sc_(s_hi)

                for fc in range(NF):
                    slot = fc % 4

                    def s_sl(sync, g=g, fc=fc, slot=slot, call=call):
                        first = (call == 0 and g == 0)
                        if fc >= 4:
                            wt(sync, "stt", sl_ms.get(("end", g, fc - 4), 0))
                            if first:
                                wt(sync, "q_w2t%d" % slot,
                                   sl_ms.get(("st", g, fc - 4), 0))
                        elif g >= 1:
                            wt(sync, "stt", grp_end[g - 1])
                            if call == 0 and g == 1:
                                wt(sync, "q_w2t%d" % slot,
                                   sl_ms.get(("st", 0, NF - 4 + fc), 0))
                        else:
                            wt(sync, "stt", mm1_end)
                            if call == 0:
                                wt(sync, "q_w2s0", n("q_w2s0"))
                                wt(sync, "q_w2s1", n("q_w2s1"))
                                wt(sync, "q_w1t0", M.get("w1t_end0", 0))
                                wt(sync, "q_w1t1", M.get("w1t_end1", 0))
                            else:
                                for kk in range(4):
                                    wt(sync, "q_w2t%d" % kk,
                                       M.get("w2t_end%d" % kk, 0))
                        sem = ("q_sl2a", "q_sl2b", "q_sl2c", "q_sl2d")[slot]
                        if first:
                            sl_ms[(g, fc)] = inc(sync.dma_start(
                                out=wbuf[:, slot * 1024:(slot + 1) * 1024],
                                in_=w2i_dram[:, fc * P:(fc + 1) * P],
                                transpose=True), sem, 16)
                            wt(sync, sem, sl_ms[(g, fc)])
                            inc(sync.dma_start(
                                out=w2iT_dram[fc * P:(fc + 1) * P, :],
                                in_=wbuf[:, slot * 1024:(slot + 1) * 1024]),
                                "q_w2t%d" % slot, 16)
                            sl_ms[("st", g, fc)] = n("q_w2t%d" % slot)
                        else:
                            sl_ms[(g, fc)] = inc(sync.dma_start(
                                out=wbuf[:, slot * 1024:(slot + 1) * 1024],
                                in_=w2iT_dram[fc * P:(fc + 1) * P, :]), sem, 16)
                    sy_(s_sl)

                    def t_fc(t, g=g, fc=fc, slot=slot):
                        if fc == 0:
                            wt(t, "ss", hi_ms[(g * 4 + 3, 1)])
                            if g >= 1:
                                wt(t, "sv", ev1max[g - 1])
                        wt(t, ("q_sl2a", "q_sl2b", "q_sl2c", "q_sl2d")[slot],
                           sl_ms[(g, fc)])
                        for ti in range(4):
                            for dh in range(2):
                                mm = t.matmul(
                                    pb[ti * 2 + dh][:, 0:512],
                                    xf[:, ti * 4096 + fc * P:
                                       ti * 4096 + (fc + 1) * P],
                                    wbuf[:, slot * 1024 + dh * 512:
                                         slot * 1024 + (dh + 1) * 512],
                                    start=(fc == 0), stop=(fc == NF - 1))
                        sl_ms[("end", g, fc)] = inc(mm, "stt")
                    te_(t_fc)
                grp_end[g] = n("stt")

                def v_ev(v, g=g):
                    wt(v, "stt", grp_end[g])
                    for half in range(2):
                        tis = (0, 1) if half == 0 else (2, 3)
                        for ti in tis:
                            for dh in range(2):
                                ssl = (ti % 2) * 2 + dh
                                v.tensor_tensor(
                                    sp[:, ssl * 512:(ssl + 1) * 512],
                                    pb[ti * 2 + dh][:, 0:512],
                                    c2B[:, dh * 512:(dh + 1) * 512], ALU.mult)
                        mev = inc(v.drain(), "sv")
                        ev1max[g] = mev
                        for ti in tis:
                            for dh in range(2):
                                ssl = (ti % 2) * 2 + dh
                                v.tensor_tensor(
                                    sp[:, ssl * 512:(ssl + 1) * 512],
                                    sp[:, ssl * 512:(ssl + 1) * 512],
                                    d2B[:, dh * 512:(dh + 1) * 512], ALU.add)
                        v.drain()
                        for ti in tis:
                            t_idx = g * 4 + ti
                            for dh in range(2):
                                ssl = (ti % 2) * 2 + dh
                                add_ms[t_idx] = inc(v.tensor_tensor(
                                    xs[:, t_idx * D + dh * 512:
                                       t_idx * D + (dh + 1) * 512],
                                    xs[:, t_idx * D + dh * 512:
                                       t_idx * D + (dh + 1) * 512],
                                    sp[:, ssl * 512:(ssl + 1) * 512],
                                    ALU.add), "sv")
                        if half == 0:
                            v.drain()
                ve_(v_ev)

                if store_y:
                    def s_y(sync, g=g):
                        for ti in range(4):
                            t_idx = g * 4 + ti
                            wt(sync, "sv", add_ms[t_idx])
                            inc(sync.dma_start(
                                out=y_out[t_idx * P:(t_idx + 1) * P, :],
                                in_=xs[:, t_idx * D:(t_idx + 1) * D]),
                                "q_y", 16)
                    sy_(s_y)

            return [add_ms[i] for i in range(NT)]

        emit_ln(0, True, None)
        mb1_loc = emit_barrier_pre(0, n("sv"))

        mask_gate = 0
        if STAGE >= 2:
            def s_mask1(s):
                wt(s, "q_sm", M["small_ld"])
                s.activation(scg[0:1, 16:17], ld0[0:1, 0:1], ACT.Sigmoid)
                s.drain()
                inc(s.activation(scg[0:1, 17:18], scg[0:1, 16:17], ACT.Ln),
                    "ss")
            sc_(s_mask1)
            m_mk1 = n("ss")

            def t_mask(t):
                wt(t, "ss", m_mk1)
                inc(t.matmul(pb[6][:, 0:1], ones_r[0:1, :], scg[0:1, 17:18],
                             start=True, stop=True), "stt")
            te_(t_mask)
            m_mk2 = n("stt")

            def s_mask2(s):
                wt(s, "stt", m_mk2)
                m_lnd = inc(s.copy(lndb[:], pb[6][:, 0:1]), "ss")
                s.drain()
                inc(s.activation(uA[:, 0:P], diffT[:], ACT.Exp, scale=lndb[:]),
                    "ss")
                M["m_lnd"] = m_lnd
            sc_(s_mask2)
            m_mk3 = n("ss")
            mask_gate = M["m_lnd"]

            def v_mask(v):
                wt(v, "ss", m_mk3)
                wt(v, "q_sm", M["small_ld"])
                inc(v.tensor_tensor(maskbf[:], uA[:, 0:P], m01[:], ALU.mult),
                    "sv")
            ve_(v_mask)
            M["mask_rdy"] = n("sv")

        w1_ld, w1_q, w1_cp, w1_st = [], [], [], []
        for i in range(NF):
            slot = i % 2

            def s_load(sync, i=i, slot=slot):
                if i >= 2:
                    wt(sync, "sv", w1_q[i - 2])
                w1_ld.append(inc(sync.dma_start(
                    out=uA[:, 2048 + slot * D:2048 + (slot + 1) * D],
                    in_=w1_in[i * P:(i + 1) * P, :]), "q_w1l%d" % slot, 16))
            sy_(s_load)

            def v_q(v, i=i, slot=slot):
                wt(v, "q_w1l%d" % slot, w1_ld[i])
                stile = uA[:, 2048 + slot * D:2048 + (slot + 1) * D]
                v.tensor_reduce(sw1[:, i:i + 1], stile, AX.X, ALU.max,
                                apply_absolute_value=True)
                v.drain()
                v.reciprocal(swi[:], sw1[:, i:i + 1])
                v.drain()
                w1_q.append(inc(v.tensor_scalar(
                    i32b[:, slot * D:(slot + 1) * D], stile, swi[:], 127.0,
                    ALU.mult, ALU.mult), "sv"))
            ve_(v_q)

            def s_cp(s, i=i, slot=slot):
                wt(s, "sv", w1_q[i])
                if i >= 2:
                    wt(s, "q_w1s%d" % slot, w1_st[i - 2])
                w1_cp.append(inc(s.copy(xf[:, slot * D:(slot + 1) * D],
                                        i32b[:, slot * D:(slot + 1) * D]), "ss"))
            sc_(s_cp)

            def s_st(sync, i=i, slot=slot):
                wt(sync, "ss", w1_cp[i])
                w1_st.append(inc(sync.dma_start(
                    out=w1i_dram[i * P:(i + 1) * P, :],
                    in_=xf[:, slot * D:(slot + 1) * D]), "q_w1s%d" % slot, 16))
            sy_(s_st)

        M["w1s0_end"] = n("q_w1s0")
        M["w1s1_end"] = n("q_w1s1")

        mb1 = emit_barrier_post(0, mb1_loc)
        b1_ms = emit_b1math(mb1)
        xb0 = emit_quant_xbar(0, True, b1_ms, 0,
                              xf_gate_dma=(("q_w1s0", M["w1s0_end"]),
                                           ("q_w1s1", M["w1s1_end"])))
        M["qx_ss_end"] = n("ss")
        hm0, mm1_end0, spl0 = emit_mm1(0, xb0, b1_ms, mask_gate, 0)
        mb2 = emit_barrier(1, hm0)
        mpb2 = emit_pb2(mb2)
        add_ffn1 = emit_mm2(0, mpb2, STAGE == 1, spl0, mm1_end0)
        mm2_end0 = n("stt")
        M["mm2end0"] = mm2_end0
        for kk in range(4):
            M["w2t_end%d" % kk] = n("q_w2t%d" % kk)
        m_mm2ev0 = n("sv")

        if STAGE == 1:
            def s_fin(sync):
                if DBG:
                    for src_ap, nm in ((sw1[:], "dbg_sw1"), (bcv[:], "dbg_bcv"),
                                       (B1v[:], "dbg_B1"), (c1s2[:], "dbg_c1s2"),
                                       (c2B[:], "dbg_c2B"), (d2B[:], "dbg_d2B"),
                                       (xf[:, 0:2048], "dbg_hint"),
                                       (scg[:], "dbg_scg"), (sw2[:], "dbg_sw2"),
                                       (b2q[:], "dbg_b2q")):
                        inc(sync.dma_start(out=dbg[nm], in_=src_ap), "q_y", 16)
                    inc(sync.dma_start(out=dbg["dbg_xint"],
                                       in_=xint_dram[0:P, :]), "q_y", 16)
                    inc(sync.dma_start(out=dbg["dbg_h"],
                                       in_=arelu_dram[0:P, 0:512]), "q_y", 16)
                wt(sync, "q_y", n("q_y"))
            sy_(s_fin)
            _check(SEM, WAITS)
            return nc

        emit_ln(NT, False, add_ffn1)
        xb_u2 = emit_quant_xbar(NT, False, add_ffn1, mm2_end0)

        def t_warm_att(t):
            wt(t, "sv", m_mm2ev0)
            for _ in range(300):
                t.matmul(pb[5][0:1, 0:P], ones_r[0:1, 0:1],
                         ones_r[0:1, 0:P], start=True, stop=True)
        te_(t_warm_att)

        AG, AQ, AK, AV, ASM = 0, 0, 4096, 8192, 12288
        uB = uA[:, :].bitcast(BF16)
        WORDER = ("wq", "wk", "wv", "wg", "wo")
        wv_ms = {}
        wlast = {}
        res_ms = {}

        def aw_load(sync, g, widx):
            k = g * 5 + widx
            slot = k % 2
            if k >= 2:
                wt(sync, "stt", wlast.get(k - 2, 0))
            if k == 0:
                wt(sync, "q_cs0", n("q_cs0"))
                wt(sync, "q_cs1", n("q_cs1"))
            if k < 2:
                wt(sync, "stt", M["mm2end0"])
            wr = wbf_dram[WORDER[widx]].rearrange("(c p) d -> p c d", p=P)
            inc(sync.dma_start(
                out=wbuf[:, slot * 8192:slot * 8192 + 4096],
                in_=wr[:, 0:4, :]), "q_aw%d" % slot, 16)
            wv_ms[k] = inc(sync.dma_start(
                out=wbuf[:, slot * 8192 + 4096:(slot + 1) * 8192],
                in_=wr[:, 4:8, :]), "q_aw%d" % slot, 16)

        def a_ld(sync):
            aw_load(sync, 0, 0)
        sy_(a_ld)

        for g in range(ATTN_NG):
            q_ev, k_ev, g_ev, v_ev2 = {}, {}, {}, {}
            stm_ms, go_ms, o_ms, st_att = {}, {}, {}, {}

            def a_ld2(sync, g=g):
                aw_load(sync, g, 1)
            sy_(a_ld2)

            def proj_feat(g, widx, dstbase, bank0, sig, ev, first_gate_ss,
                          first_gate_sv, in_uB=False):
                k = g * 5 + widx
                slot = k % 2
                for dc in range(NE):
                    def t_p(t, dc=dc, k=k, slot=slot, g=g, widx=widx,
                            bank0=bank0):
                        bk = bank0 + (dc % 2)
                        if dc == 0:
                            wt(t, "q_aw%d" % (k % 2), wv_ms[k])
                            wt(t, "q_xb", xb_u2[-1])
                            if first_gate_ss:
                                wt(t, "ss", first_gate_ss)
                            if first_gate_sv:
                                wt(t, "sv", first_gate_sv)
                        if dc >= 2:
                            wt(t, "ss", ev[dc - 2])
                        for ei in range(NE):
                            mm = t.matmul(
                                pb[bk][:, 0:512],
                                wbuf[:, slot * 8192 + ei * D + dc * P:
                                     slot * 8192 + ei * D + (dc + 1) * P],
                                xf[:, ei * T + g * 512:ei * T + (g + 1) * 512],
                                start=(ei == 0), stop=(ei == NE - 1))
                        ev[("mm", dc)] = inc(mm, "stt")
                        wlast[k] = n("stt")
                    te_(t_p)

                    def s_p(s, dc=dc, dstbase=dstbase, bank0=bank0, sig=sig,
                            ev=ev, in_uB=in_uB):
                        bk = bank0 + (dc % 2)
                        wt(s, "stt", ev[("mm", dc)])
                        buf = uB if in_uB else bfB
                        dst = buf[:, dstbase + dc * 512:dstbase + (dc + 1) * 512]
                        if sig:
                            op = s.activation(dst, pb[bk][:, 0:512], ACT.Sigmoid)
                        else:
                            op = s.copy(dst, pb[bk][:, 0:512])
                        ev[dc] = inc(op, "ss")
                    sc_(s_p)

            gate0_sv = m_mm2ev0 if g == 0 else res_ms.get((g - 1) * 4 + 3, m_mm2ev0)
            if ATTN_PARTS < 1:
                continue
            proj_feat(g, 0, AQ, 0, False, q_ev, 0, gate0_sv, in_uB=True)

            def a_ld3(sync, g=g):
                aw_load(sync, g, 2)
            sy_(a_ld3)

            proj_feat(g, 1, AK, 2, False, k_ev, 0, gate0_sv, in_uB=True)

            def a_ld4(sync, g=g):
                aw_load(sync, g, 3)
            sy_(a_ld4)

            k = g * 5 + 2
            slotv = k % 2
            for tsub in range(4):
                for dh in range(2):
                    def t_v(t, g=g, tsub=tsub, dh=dh, k=k, slotv=slotv,
                            gate0_sv=gate0_sv):
                        bk = 4 + (tsub % 2) * 2 + dh
                        if tsub == 0 and dh == 0:
                            wt(t, "q_aw%d" % (k % 2), wv_ms[k])
                            wt(t, "sv", gate0_sv)
                        if tsub >= 2:
                            wt(t, "ss", v_ev2[(tsub - 2, dh)])
                        for ei in range(NE):
                            mm = t.matmul(
                                pb[bk][:, 0:512],
                                xf[:, ei * T + g * 512 + tsub * P:
                                   ei * T + g * 512 + (tsub + 1) * P],
                                wbuf[:, slotv * 8192 + ei * D + dh * 512:
                                     slotv * 8192 + ei * D + (dh + 1) * 512],
                                start=(ei == 0), stop=(ei == NE - 1))
                        v_ev2[("mm", tsub, dh)] = inc(mm, "stt")
                        wlast[k] = n("stt")
                    te_(t_v)

                    def s_v(s, g=g, tsub=tsub, dh=dh):
                        bk = 4 + (tsub % 2) * 2 + dh
                        wt(s, "stt", v_ev2[("mm", tsub, dh)])
                        v_ev2[(tsub, dh)] = inc(s.copy(
                            uB[:, AV + tsub * D + dh * 512:
                               AV + tsub * D + (dh + 1) * 512],
                            pb[bk][:, 0:512]), "ss")
                    sc_(s_v)

            def a_ld5(sync, g=g):
                aw_load(sync, g, 4)
            sy_(a_ld5)

            proj_feat(g, 3, AG, 0, True, g_ev, q_ev[NE - 1], 0)

            if ATTN_PARTS < 2:
                continue
            for w in range(4):
                def t_st(t, g=g, w=w):
                    wt(t, "ss", max(q_ev[NE - 1], k_ev[NE - 1]))
                    wt(t, "ss", v_ev2[(3, 0)])
                    if w >= 1:
                        wt(t, "sv", max(go_ms.get((w - 1, 0), stm_ms[w - 1]),
                                        go_ms.get((w - 1, 1), stm_ms[w - 1])))
                    mm = None
                    for dc in range(NE):
                        mm = t.matmul(
                            pb[6][:, 0:P],
                            uB[:, AK + dc * 512 + w * P:AK + dc * 512 + (w + 1) * P],
                            uB[:, AQ + dc * 512 + w * P:AQ + dc * 512 + (w + 1) * P],
                            start=(dc == 0), stop=(dc == NE - 1))
                    st_att[w] = inc(mm, "stt")
                te_(t_st)

                def v_stm(v, g=g, w=w):
                    wt(v, "stt", st_att[w])
                    if w == 0 and g == 0:
                        wt(v, "sv", M["mask_rdy"])
                    stm_ms[w] = inc(v.tensor_tensor(
                        uB[:, ASM + w * P:ASM + (w + 1) * P],
                        pb[6][:, 0:P], maskbf[:], ALU.mult), "sv")
                ve_(v_stm)

                if ATTN_PARTS < 3:
                    continue
                for half in range(2):
                    def t_o(t, g=g, w=w, half=half):
                        bnk = 7 if half == 0 else 6
                        if half == 0:
                            wt(t, "sv", stm_ms[w])
                            wt(t, "ss", v_ev2[(3, 1)])
                            if w >= 1:
                                wt(t, "sv", go_ms[(w - 1, 1)])
                        else:
                            wt(t, "sv", stm_ms[w])
                            if w >= 1:
                                wt(t, "sv", go_ms[(w - 1, 0)])
                        mm = None
                        for k in range(4):
                            dc = half * 4 + k
                            mm = t.matmul(
                                pb[bnk][:, k * P:(k + 1) * P],
                                uB[:, AV + w * D + dc * P:AV + w * D + (dc + 1) * P],
                                uB[:, ASM + w * P:ASM + (w + 1) * P],
                                start=True, stop=True)
                        o_ms[(w, half)] = inc(mm, "stt")
                    te_(t_o)

                    def v_go(v, g=g, w=w, half=half):
                        bnk = 7 if half == 0 else 6
                        wt(v, "stt", o_ms[(w, half)])
                        if w == 0 and half == 0:
                            wt(v, "ss", g_ev[NE - 1])
                        for k in range(4):
                            dc = half * 4 + k
                            gsl = bfB[:, AG + dc * 512 + w * P:
                                      AG + dc * 512 + (w + 1) * P]
                            mm = v.tensor_tensor(gsl, pb[bnk][:, k * P:(k + 1) * P],
                                                 gsl, ALU.mult)
                        go_ms[(w, half)] = inc(mm, "sv")
                    ve_(v_go)

            if ATTN_PARTS < 4:
                continue
            k = g * 5 + 4
            slotw = k % 2
            womm = {}
            for tsub in range(4):
                def t_wo(t, g=g, tsub=tsub, k=k, slotw=slotw):
                    if tsub == 0:
                        wt(t, "q_aw%d" % (k % 2), wv_ms[k])
                        wt(t, "sv", max(go_ms[(w, h)] for w in range(4)
                                        for h in range(2)))
                        wt(t, "ss", g_ev[NE - 1])
                    if tsub >= 2:
                        wt(t, "sv", res_ms[("h", g * 4 + tsub - 2, 0)])
                        wt(t, "sv", res_ms[("h", g * 4 + tsub - 2, 1)])
                    for dh in range(2):
                        bk = (tsub % 2) * 2 + dh
                        for dc in range(NE):
                            mm = t.matmul(
                                pb[bk][:, 0:512],
                                bfB[:, AG + dc * 512 + tsub * P:
                                     AG + dc * 512 + (tsub + 1) * P],
                                wbuf[:, slotw * 8192 + dc * D + dh * 512:
                                     slotw * 8192 + dc * D + (dh + 1) * 512],
                                start=(dc == 0), stop=(dc == NE - 1))
                        womm[(tsub, dh)] = inc(mm, "stt")
                    wlast[k] = n("stt")
                te_(t_wo)

                def v_res(v, g=g, tsub=tsub):
                    t_idx = g * 4 + tsub
                    ut = sp[:, (tsub % 2) * D:(tsub % 2 + 1) * D]
                    wt(v, "stt", womm[(tsub, 1)])
                    v.tensor_scalar(ut, xs[:, t_idx * D:(t_idx + 1) * D],
                                    mvec[:, NT + t_idx:NT + t_idx + 1],
                                    rvec[:, NT + t_idx:NT + t_idx + 1],
                                    ALU.subtract, ALU.mult)
                    v.drain()
                    for dh in range(2):
                        bk = (tsub % 2) * 2 + dh
                        res_ms[("h", t_idx, dh)] = inc(v.tensor_tensor(
                            ut[:, dh * 512:(dh + 1) * 512],
                            pb[bk][:, 0:512],
                            ut[:, dh * 512:(dh + 1) * 512], ALU.add), "sv")
                    v.drain()
                    res_ms[t_idx] = inc(v.tensor_tensor(
                        xs[:, t_idx * D:(t_idx + 1) * D],
                        xs[:, t_idx * D:(t_idx + 1) * D], ut, ALU.add), "sv")
                ve_(v_res)

            if g < 3:
                def a_ld6(sync, g=g):
                    aw_load(sync, g + 1, 0)
                sy_(a_ld6)

        attn_end = n("stt")
        res_list = [res_ms.get(i, n("sv")) for i in range(NT)]

        if STAGE == 2:
            def s_fin2(sync):
                for i in range(NT):
                    wt(sync, "sv", res_list[i])
                    inc(sync.dma_start(out=y_out[i * P:(i + 1) * P, :],
                                       in_=xs[:, i * D:(i + 1) * D]),
                        "q_y", 16)
                wt(sync, "q_y", n("q_y"))
            sy_(s_fin2)
            _check(SEM, WAITS)
            return nc

        emit_ln(2 * NT, True, res_list)
        mb3 = emit_barrier(2, n("sv"))
        b1_ms2 = emit_b1math(mb3)
        xb1 = emit_quant_xbar(2 * NT, True, b1_ms2, attn_end)
        hm1, mm1_end1, spl1 = emit_mm1(1, xb1, b1_ms2, 0, attn_end)
        mb4 = emit_barrier(3, hm1)
        mpb4 = emit_pb2(mb4)
        emit_mm2(1, mpb4, True, spl1, mm1_end1)

        def s_fin3(sync):
            wt(sync, "q_y", n("q_y"))
        sy_(s_fin3)
        _check(SEM, WAITS)

    return nc


_CACHE = {}


def _prep(inputs):
    xfull = np.ascontiguousarray(np.asarray(inputs["L0_attn_x"], dtype=np.float32))
    w1 = np.ascontiguousarray(np.asarray(inputs["L0_ffn_w1"], dtype=np.float32))
    w2 = np.ascontiguousarray(np.asarray(inputs["L0_ffn_w2"], dtype=np.float32))
    b1 = np.asarray(inputs["L0_ffn_b1"], dtype=np.float32)
    b2 = np.asarray(inputs["L0_ffn_b2"], dtype=np.float32)
    b1r = np.ascontiguousarray(b1.reshape(NF, P).T)
    b2r = np.ascontiguousarray(b2.reshape(NE, P).T)
    wT = {w: np.ascontiguousarray(np.asarray(inputs[k], dtype=np.float32).T)
          for w, k in (("wq", "L0_attn_W_q"), ("wk", "L0_attn_W_k"),
                       ("wv", "L0_attn_W_v"), ("wg", "L0_attn_W_g"),
                       ("wo", "L0_attn_W_o"))}
    ld0 = np.ascontiguousarray(
        np.asarray(inputs["L0_attn_log_decay"], dtype=np.float32)[0:1]
        .reshape(1, 1))
    kk = np.arange(P)
    cc = np.arange(P)
    diff = cc[None, :] - kk[:, None]
    valid = (cc[None, :] // 32 == kk[:, None] // 32) & (diff >= 0)
    diffT = np.where(valid, diff, 0).astype(np.float32)
    m01v = valid.astype(np.float32)
    id32v = np.eye(P, dtype=np.float32)
    per_core = []
    for c in range(NCORE):
        m = {"x": np.ascontiguousarray(xfull[c]), "w1": w1, "w2": w2,
             "b1r": b1r, "b2r": b2r, "ld0": ld0, "diffT": diffT,
             "m01": m01v, "id32": id32v}
        for w in wT:
            m[w + "T"] = wT[w]
        per_core.append(m)
    return per_core


def _check(SEM, WAITS):
    bad = [(nm, v, SEM[nm][1]) for nm, v in WAITS if v > SEM[nm][1]]
    if bad:
        raise RuntimeError("unsatisfiable waits: %s" % bad[:10])


def kernel(**inputs):
    if "nc" not in _CACHE:
        _CACHE["nc"] = build()
    nc = _CACHE["nc"]
    in_maps = _prep(inputs)
    res = run_bass_kernel_spmd(nc, in_maps, core_ids=list(range(NCORE)))
    return np.stack([res.results[i]["y"] for i in range(NCORE)], axis=0)
